# version 63
# baseline (speedup 1.0000x reference)
import numpy as np
import ml_dtypes
from contextlib import ExitStack

import concourse.bass as bass
import concourse.mybir as mybir
from concourse.bass_utils import run_bass_kernel_spmd

F32 = mybir.dt.float32
BF16 = mybir.dt.bfloat16
AF = mybir.ActivationFunctionType
ALU = mybir.AluOpType
AX = mybir.AxisListType

P = 128
D = 1024
F = 4096
E = 2048
S = 16
HH = 4096
TC = 2048
HALO = 256
TTOK = TC + HALO
NDC, NFC, NEC, NHC = 8, 32, 16, 32
NT = TTOK // P
GRP = 256
NG = TTOK // GRP
NMT = TC // P
NELEM_W = 4096 * 1024

ENGS = ("sync", "gpsimd", "tensor", "vector", "scalar")
_DEBUG = False


class Plan:
    DMA_SLOTS = 10

    def __init__(self):
        self.stream = {e: [] for e in ENGS}
        self.count = {}
        self.last_wait = {e: {} for e in ENGS}
        self.dma_idx = {}

    def sem(self, name):
        self.count[name] = 0

    def op(self, eng, fn, waits=(), inc=None, amt=1):
        w = []
        for wv in waits:
            if wv is None:
                continue
            s, v = wv
            if v is None or v <= 0:
                continue
            if self.last_wait[eng].get(s, -1) >= v:
                continue
            self.last_wait[eng][s] = v
            w.append((s, v))
        self.stream[eng].append((w, fn, (inc, amt) if inc else None))
        if inc is not None:
            self.count[inc] += amt
            return self.count[inc]
        return None

    def chain(self, eng, fns, waits=(), inc="dve"):
        w = []
        for wv in waits:
            if wv is None:
                continue
            s, v = wv
            if v is None or v <= 0:
                continue
            if self.last_wait[eng].get(s, -1) >= v:
                continue
            self.last_wait[eng][s] = v
            w.append((s, v))
        base = self.count[inc]
        self.count[inc] += len(fns)
        self.last_wait[eng][inc] = self.count[inc]
        self.stream[eng].append((w, ("chain", fns, base, inc), None))
        return self.count[inc]

    def dma(self, eng, fn, waits=(), pool="qx"):
        k = self.dma_idx.get(pool, 0)
        self.dma_idx[pool] = k + 1
        sem = f"{pool}{k % self.DMA_SLOTS}"
        w = list(waits)
        if k >= self.DMA_SLOTS:
            w.append((sem, 16 * (k // self.DMA_SLOTS)))
        cnt = self.op(eng, fn, w, inc=sem, amt=16)
        return (sem, cnt)

    def snap(self):
        return dict(self.count)

    def barrier(self, marks):
        for e in ENGS:
            w = []
            for s, v in marks.items():
                if v and self.last_wait[e].get(s, -1) < v:
                    self.last_wait[e][s] = v
                    w.append((s, v))
            if w:
                self.stream[e].append((w, None, None))


def build_nc():
    nc = bass.Bass("TRN2", target_bir_lowering=False)

    d_x = nc.dram_tensor("xin", [TTOK, D], F32, kind="ExternalInput").ap()
    d_wint = nc.dram_tensor("w_in_t", [F, D], F32, kind="ExternalInput").ap()
    d_woutt = nc.dram_tensor("w_out_t", [E, D], F32, kind="ExternalInput").ap()
    d_wfct = nc.dram_tensor("w_fc_t", [HH, D], F32, kind="ExternalInput").ap()
    d_wpjt = nc.dram_tensor("w_proj_t", [HH, D], F32, kind="ExternalInput").ap()
    d_wbt = nc.dram_tensor("w_b_t", [P, NEC * S], F32, kind="ExternalInput").ap()
    d_wct = nc.dram_tensor("w_c_t", [S, E], F32, kind="ExternalInput").ap()
    d_par = nc.dram_tensor("par_fm", [P, NFC + NEC + NHC], F32,
                           kind="ExternalInput").ap()
    d_bout = nc.dram_tensor("b_out_row", [1, D], F32, kind="ExternalInput").ap()
    d_lnb = nc.dram_tensor("lnb_fm", [P, NHC], F32, kind="ExternalInput").ap()
    d_mask = nc.dram_tensor("mask16", [S, 1], F32, kind="ExternalInput").ap()
    d_idn = nc.dram_tensor("ident", [P, P], BF16, kind="ExternalInput").ap()
    d_out = nc.dram_tensor("out", [TC, D], F32, kind="ExternalOutput").ap()
    d_sspill = nc.dram_tensor("s_spill", [TC, D], BF16, kind="Internal").ap()
    d_gspill = nc.dram_tensor("g_spill", [TC, D], BF16, kind="Internal").ap()
    d_wq1 = nc.dram_tensor("wq1_img", [NDC * 4 * P, 1024], BF16, kind="Internal").ap()
    d_wq2 = nc.dram_tensor("wq2_img", [NHC * P, 1024], BF16, kind="Internal").ap()

    dbg = {}
    if _DEBUG:
        for nm, shape, dt in [
            ("xnt", [P, NDC * TTOK], BF16), ("bt", [S, TTOK], BF16),
            ("hst", [S, TTOK], BF16), ("thr", [P, 4], F32),
            ("v2b", [P, D], BF16), ("cs2b", [P, D], BF16),
            ("mut", [P, NMT], F32), ("rst", [P, NMT], F32),
        ]:
            dbg[nm] = nc.dram_tensor("dbg_" + nm, shape, dt, kind="ExternalOutput").ap()

    es = ExitStack()

    def sb(name, shape, dt):
        return es.enter_context(nc.sbuf_tensor(name, shape, dt))

    XNT = sb("XNT", [P, NDC * TTOK], BF16)
    BW1 = sb("BW1", [P, NDC * F], BF16)
    BW2 = sb("BW2", [P, 32768], BF16)
    SCRB = sb("SCRB", [P, 6656], BF16)
    SCRF = sb("SCRF", [P, 2048], F32)
    SCANB = sb("SCANB", [P, 5632], BF16)
    F3S = sb("F3S", [P, 2048], BF16)
    F3X = sb("F3X", [P, 1024], F32)
    F3SF = F3S.bitcast(F32)
    SCANBF = SCANB.bitcast(F32)
    WBT = F3S[:, 0:NEC * S]
    DEC9 = sb("DEC9", [S, GRP], F32)
    PAR = sb("PAR", [P, NFC + NEC + NHC], F32)
    BIN = PAR[:, 0:NFC]
    DFM = PAR[:, NFC:NFC + NEC]
    LNW = PAR[:, NFC + NEC:NFC + NEC + NHC]
    LNBbf = sb("LNBbf", [P, NHC], BF16)
    MASK = sb("MASK", [S, 1], F32)
    IDN = sb("IDN", [P, P], BF16)
    ONESR = F3S[0:1, 256:256 + P]
    ONESRF = F3X[0:1, 0:P]
    ONESC = sb("ONESC", [P, 1], BF16)
    ONESCF = sb("ONESCF", [P, 1], F32)
    GACC = F3X[:, 138:140]
    THRROW = F3X[0:1, 128:132]
    THRTMP = F3X[0:1, 136:138]
    THR = F3X[:, 132:136]
    GSQ = sb("GSQ", [P, 16], F32)
    GS, QS = GSQ[:, 0:8], GSQ[:, 8:16]
    STF = sb("STF", [P, 6 * NMT], F32)
    GPARTS = F3X[:, 144:208]
    MUT, SQT, VET = STF[:, 0:16], STF[:, 16:32], STF[:, 32:48]
    RST, NRM, NTA = STF[:, 48:64], STF[:, 64:80], STF[:, 80:96]

    PS = [es.enter_context(nc.psum_tensor(f"pb{i}", [P, 512], F32)) for i in range(8)]

    def B(c0, n):
        return SCRB[:, c0:c0 + n]

    def Fv(c0, n):
        return SCRF[:, c0:c0 + n]

    def BTb(c0, n):
        return SCANB[0:S, c0:c0 + n]

    def HSTb(c0, n):
        return SCANB[0:S, 2304 + c0: 2304 + c0 + n]

    def WCTb(ec):
        return BW2[0:S, 30720 + ec * P: 30720 + (ec + 1) * P]

    def W2u(c0, n):
        return BW2[:, 20480 + c0: 20480 + c0 + n]

    BOUTBF = BW2[0:1, 29696:30720]
    VROW = SCRF[0:64, 0:1024]
    V2B = SCANB[:, 0:1024]
    CS2B = SCANB[:, 1024:2048]

    sems = {}
    semnames = ["pe", "dve", "act", "pool", "qmask", "qsm", "qmv"]
    for pool_ in ("qx", "qw", "qo"):
        for k in range(Plan.DMA_SLOTS):
            semnames.append(f"{pool_}{k}")
    for s_ in semnames:
        sems[s_] = es.enter_context(nc.semaphore(s_))

    pl = Plan()
    for s_ in sems:
        pl.sem(s_)

    def ld(dst, src, waits=()):
        return pl.dma("sync", lambda e, d=dst, s=src: e.dma_start(out=d, in_=s),
                      waits=waits, pool="qx")

    mk_par = ld(PAR[:], d_par)
    mk_bin = mk_dfm = mk_lnw = mk_par
    mk_idn = ld(IDN[:], d_idn)
    pl.op("sync", lambda e: e.dma_start(out=MASK[:], in_=d_mask),
          inc="qmask", amt=16)
    q_params = [mk_par, mk_idn]

    def cld(dst, src):
        return pl.dma("gpsimd", lambda e, d=dst, s=src: e.dma_start(out=d, in_=s),
                      pool="qw")

    q_lnbbf = cld(LNBbf[:], d_lnb)
    pl.op("gpsimd", lambda e: e.dma_start(out=BOUTBF, in_=d_bout), inc="qsm", amt=16)
    pl.op("gpsimd", lambda e: e.dma_start(out=BW2[0:S, 30720:32768], in_=d_wct),
          inc="qsm", amt=16)
    q_small = ("qsm", 32)
    q_mask = ("qmask", 16)
    q_win = {}
    q_wout = {}

    d_memset = pl.chain("vector", [
        (lambda e, a=ap_, vv=v_: e.memset(a, vv))
        for ap_, v_ in ((DEC9[:], 0.9), (ONESC[:], 1.0), (ONESCF[:], 1.0))
    ])

    gpieces = []
    for c in range(NDC):
        for j in range(4):
            pk = c * 4 + j
            gpieces.append((0, d_wfct[pk * P:(pk + 1) * P, :]))
    for c in range(NHC):
        gpieces.append((1, d_wpjt[c * P:(c + 1) * P, :]))
    n_gp = len(gpieces)
    MQ = 16384
    m_gin = {}
    m_gabs = {}
    ga_slot = {}

    def emit_gamma_load(i):
        if i >= n_gp:
            return
        w_, src_ = gpieces[i]
        k, buf, kwaits = xf_alloc()
        ga_slot[i] = (k, buf)
        m_gin[i] = pl.dma("sync", lambda e, d=buf, s=src_: e.dma_start(out=d, in_=s),
                          waits=kwaits, pool="qx")

    def emit_gamma_abs(i):
        k, buf = ga_slot[i]
        m_gabs[i] = pl.op(
            "scalar",
            lambda e, d=buf, i=i: e.activation(out=BW2[:, MQ:MQ + 1024], in_=d,
                                               func=AF.Abs,
                                               accum_out=GPARTS[:, i:i + 1]),
            waits=[m_gin[i], ("act", m_gabs.get(i - 1)),
                   ("act", m_sqln[NT - 1]) if i == 0 else None,
                   ("dve", m_xnb[NT - 1]) if i == 0 else None], inc="act")
        xf_cons[k] = [("act", m_gabs[i])]

    def emit_gamma_tick():
        j = ga_state["j"]
        if j >= n_gp:
            return
        emit_gamma_abs(j)
        emit_gamma_load(j + 1)
        ga_state["j"] = j + 1

    ga_state = {"j": 0}

    xf_state = {"k": 0}
    xf_cons = {}

    def xf_alloc():
        k = xf_state["k"]
        xf_state["k"] += 1
        assert k < 2 or (k - 2) in xf_cons, f"xf slot {k} missing consumer"
        return k, Fv(1024 * (k % 2), 1024), list(xf_cons.get(k - 2, ()))

    SXL, SQL = STF[:, 0:NT], STF[:, 18:18 + NT]
    NMU, RSL = STF[:, 36:36 + NT], STF[:, 54:54 + NT]
    TMPL = STF[:, 72:72 + NT]
    m_xload = {}
    m_red = {}
    m_sqln = {}
    m_xnb = {}
    m_l2b = {}
    m_trev_d = {}
    m_trev_s = {}
    m_trpe = {}
    XL1 = [F3X[:, 0:1024], F3SF[:, 0:1024], SCANBF[:, 1024:2048]]
    TRB = [(PS[0], PS[1]), (PS[2], PS[3]), (PS[4], PS[5])]
    m_l2a = {}

    m_cast = {}

    def emit_ln_part1(t):
        xt = XL1[t % 3]
        m_xload[t] = pl.dma(
            "sync", lambda e, d=xt, s=d_x[t * P:(t + 1) * P, :]: e.dma_start(out=d, in_=s),
            waits=[("pool", m_cast.get(t - 3)), ("act", m_sqln.get(t - 3))],
            pool="qx")
        m_red[t] = pl.op(
            "scalar",
            lambda e, x=xt, t=t: e.activation(out=BW2[:, MQ:MQ + 1024], in_=x,
                                              func=AF.Copy, accum_out=SXL[:, t:t + 1]),
            waits=[m_xload[t], ("act", m_sqln.get(t - 1))], inc="act")
        m_sqln[t] = pl.op(
            "scalar",
            lambda e, x=xt, t=t: e.activation(out=BW2[:, MQ:MQ + 1024], in_=x,
                                              func=AF.Square, accum_out=SQL[:, t:t + 1]),
            waits=[("act", m_red[t])], inc="act")
        sx, sq = SXL[:, t:t + 1], SQL[:, t:t + 1]
        tm = TMPL[:, t:t + 1]
        m_l2a[t] = pl.chain("vector", [
            lambda e, sx=sx: e.tensor_scalar(out=sx, in0=sx, scalar1=1.0 / D,
                                             scalar2=None, op0=ALU.mult),
            lambda e, sq=sq: e.tensor_scalar(out=sq, in0=sq, scalar1=1.0 / D,
                                             scalar2=None, op0=ALU.mult),
            lambda e, sx=sx, tm=tm: e.tensor_tensor(out=tm, in0=sx, in1=sx,
                                                    op=ALU.mult),
            lambda e, sq=sq, tm=tm: e.tensor_tensor(out=sq, in0=sq, in1=tm,
                                                    op=ALU.subtract),
            lambda e, sq=sq: e.tensor_scalar(out=sq, in0=sq, scalar1=1e-5,
                                             scalar2=None, op0=ALU.add),
        ], waits=[("act", m_sqln[t])])
        m_cast[t] = pl.op(
            "gpsimd",
            lambda e, x=xt, t=t: e.tensor_copy(out=B(1024 * (t % 4), 1024), in_=x),
            waits=[m_xload[t], ("pe", m_trpe.get(t - 4)),
                   ("pool", m_cast.get(t - 4))], inc="pool")

    def emit_ln_part2(t):
        sx, sq = SXL[:, t:t + 1], SQL[:, t:t + 1]
        tm = TMPL[:, t:t + 1]
        nm, rs = NMU[:, t:t + 1], RSL[:, t:t + 1]
        sd = pl.op("scalar",
                   lambda e, sq=sq, tm=tm: e.activation(out=tm, in_=sq, func=AF.Sqrt),
                   waits=[("dve", m_l2a[t])], inc="act")
        m_l2b[t] = pl.chain("vector", [
            lambda e, rs=rs, tm=tm: e.reciprocal(out=rs, in_=tm),
            lambda e, sx=sx, nm=nm: e.tensor_scalar(out=nm, in0=sx, scalar1=-1.0,
                                                    scalar2=None, op0=ALU.mult),
        ], waits=[("act", sd)])
        xnb = B(1024 * (t % 4), 1024)
        m_xnb[t] = pl.op(
            "vector",
            lambda e, o=xnb, t=t: e.tensor_scalar(
                out=o, in0=o, scalar1=NMU[:, t:t + 1], scalar2=RSL[:, t:t + 1],
                op0=ALU.add, op1=ALU.mult),
            waits=[("dve", m_l2b[t]), ("pool", m_cast[t])], inc="dve")
        banks = TRB[t % 3]

        def tr(e, xnb=xnb, banks=banks):
            last = None
            for c in range(NDC):
                bank = banks[c // 4]
                off = (c % 4) * P
                last = e.matmul(bank[:, off:off + P],
                                xnb[:, c * P:(c + 1) * P], IDN[:],
                                start=True, stop=True)
            return last
        m_trpe[t] = pl.op("tensor", tr,
                          waits=[("dve", m_xnb[t]), mk_idn,
                                 ("dve", m_trev_d.get(t - 3)),
                                 ("act", m_trev_s.get(t - 3))], inc="pe")

    def emit_ln_part3(t):
        banks = TRB[t % 3]

        def xv(bi, t=t, banks=banks):
            return XNT[:].rearrange("p (c tt) -> p c tt", c=NDC)[
                :, bi * 4:(bi + 1) * 4, t * P:(t + 1) * P]
        m_trev_d[t] = pl.op(
            "vector",
            lambda e, t=t, banks=banks: e.tensor_copy(
                out=xv(0), in_=banks[0][:].rearrange("p (c f) -> p c f", c=4)),
            waits=[("pe", m_trpe[t])], inc="dve")
        m_trev_s[t] = pl.op(
            "scalar",
            lambda e, t=t, banks=banks: e.activation(
                out=xv(1), in_=banks[1][:].rearrange("p (c f) -> p c f", c=4),
                func=AF.Copy),
            waits=[("pe", m_trpe[t])], inc="act")

    for t in range(NT):
        if t >= 2:
            emit_ln_part2(t - 2)
        if t >= 3:
            emit_ln_part3(t - 3)
        emit_ln_part1(t)
    for t_ in (NT - 2, NT - 1):
        emit_ln_part2(t_)
    for t_ in (NT - 3, NT - 2, NT - 1):
        emit_ln_part3(t_)

    nwp = [0]
    _pc = {}
    for q in range(4):
        for c in range(NDC):
            w_ = []
            if q == 0 and c == 0:
                w_ = [("act", m_sqln[NT - 1])]
            elif nwp[0] >= 16:
                w_ = [q_win[(nwp[0] - 2) % 4 * 0 + _pc[nwp[0] - 2]]]
            q_win[c * 4 + q] = pl.dma(
                "gpsimd",
                lambda e, c=c, q=q: e.dma_start(
                    out=BW1[:, c * F + q * 1024:c * F + (q + 1) * 1024],
                    in_=d_wint[(c * 4 + q) * P:(c * 4 + q + 1) * P, :]),
                waits=w_, pool="qw")
            _pc[nwp[0]] = c * 4 + q
            nwp[0] += 1
    for j in range(NEC):
        q_wout[j] = pl.dma(
            "gpsimd",
            lambda e, j=j: e.dma_start(out=BW2[:, j * D:(j + 1) * D],
                                       in_=d_woutt[j * P:(j + 1) * P, :]),
            waits=[q_wout[j - 2]] if j >= 2 else [], pool="qw")

    wbt_f32 = SCANBF[:, 2048:2304]
    m_wbt_ld = pl.dma("sync", lambda e: e.dma_start(out=wbt_f32, in_=d_wbt),
                      pool="qx")
    m_wbt = pl.op("vector", lambda e: e.tensor_copy(out=WBT[:], in_=wbt_f32),
                  waits=[m_wbt_ld, ("act", m_sqln[16]), ("dve", m_xnb[16])],
                  inc="dve")
    m_ones2 = pl.chain("vector", [
        lambda e: e.memset(ONESR[:], 1.0),
        lambda e: e.memset(ONESRF[:], 1.0),
    ], waits=[("act", m_sqln[16]), ("dve", m_xnb[16])])

    emit_gamma_load(0)


    DGM0 = 4096

    def dgm_ops():
        fns = []
        for ec in range(NEC):
            fns.append(lambda e, ec=ec: e.tensor_scalar(
                out=B(DGM0 + ec * P, P), in0=IDN[:], scalar1=DFM[:, ec:ec + 1],
                scalar2=None, op0=ALU.mult))
        return fns
    m_dgm = pl.chain("vector", dgm_ops(), waits=[mk_idn, mk_dfm])

    qpieces = [("fc", c, j) for c in range(NDC) for j in range(4)] + \
              [("pj", c, 0) for c in range(NHC)]
    n_qp = len(qpieces)
    m_qin = {}
    m_qcmp = {}
    m_qst = {}
    q_pending = []
    wq_ping = [MQ, MQ + 1024, MQ + 2048]
    NWQ = 3
    thr_emitted = [False]
    m_thr = [None]

    def emit_thr():
        m_gred = pl.chain("vector", [
            lambda e: e.reduce_sum(out=GACC[:, 0:1], in_=GPARTS[:, 0:32], axis=AX.X),
            lambda e: e.reduce_sum(out=GACC[:, 1:2], in_=GPARTS[:, 32:64], axis=AX.X),
        ], waits=[("act", m_gabs[n_gp - 1]), ("dve", d_memset)])
        mm1t = pl.op("tensor",
                     lambda e: e.matmul(PS[7][0:1, 0:2], ONESCF[:], GACC[:],
                                        start=True, stop=True),
                     waits=[("dve", m_gred)], inc="pe")

        mt_ = pl.chain("vector", [
            lambda e: e.tensor_copy(out=THRTMP[:], in_=PS[7][0:1, 0:2]),
            lambda e: e.tensor_scalar(out=THRTMP[:], in0=THRTMP[:], scalar1=1.0 / NELEM_W,
                                      scalar2=1e-6, op0=ALU.mult, op1=ALU.add),
            lambda e: e.tensor_scalar(out=THRROW[0:1, 0:2], in0=THRTMP[:], scalar1=0.5,
                                      scalar2=None, op0=ALU.mult),
            lambda e: e.tensor_scalar(out=THRROW[0:1, 2:4], in0=THRTMP[:], scalar1=-0.5,
                                      scalar2=None, op0=ALU.mult),
        ], waits=[("pe", mm1t)])
        mb = pl.op("tensor",
                   lambda e: e.matmul(PS[7][:, 0:4], ONESRF[:], THRROW[:],
                                      start=True, stop=True),
                   waits=[("dve", mt_)], inc="pe")
        m_thr[0] = pl.op("vector", lambda e: e.tensor_copy(out=THR[:], in_=PS[7][:, 0:4]),
                         waits=[("pe", mb)], inc="dve")

    def emit_quant_piece(i):
        if not thr_emitted[0]:
            emit_thr()
            thr_emitted[0] = True
        kind, c, j = qpieces[i]
        if kind == "fc":
            pk = c * 4 + j
            src = d_wfct[pk * P:(pk + 1) * P, :]
            dst = d_wq1[pk * P:(pk + 1) * P, :]
            thr_c, nthr_c = 0, 2
        else:
            src = d_wpjt[c * P:(c + 1) * P, :]
            dst = d_wq2[c * P:(c + 1) * P, :]
            thr_c, nthr_c = 1, 3
        k, qin, kwaits = xf_alloc()
        wq = BW2[:, wq_ping[i % NWQ]:wq_ping[i % NWQ] + 1024]
        m_qin[i] = pl.dma("sync", lambda e, d=qin, s=src: e.dma_start(out=d, in_=s),
                          waits=kwaits, pool="qx")
        nn_ = BW2[:, MQ + 3072:MQ + 4096]
        m_qcmp[i] = pl.chain("vector", [
            lambda e, qin=qin, nthr_c=nthr_c: e.tensor_single_scalar(
                out=nn_, in_=qin, scalar=THR[:, nthr_c:nthr_c + 1], op=ALU.is_le),
            lambda e, qin=qin, wq=wq, thr_c=thr_c: e.scalar_tensor_tensor(
                out=wq, in0=qin, scalar=THR[:, thr_c:thr_c + 1], in1=nn_,
                op0=ALU.is_ge, op1=ALU.subtract),
        ], waits=[("dve", m_thr[0]), m_qin[i], m_qst.get(i - NWQ)])
        xf_cons[k] = [("dve", m_qcmp[i])]
        q_pending.append((i, dst, wq))

    def emit_quant_store():
        if not q_pending:
            return
        i, dst, wq = q_pending.pop(0)
        m_qst[i] = pl.dma("sync", lambda e, d=dst, s=wq: e.dma_start(out=d, in_=s),
                          waits=[("dve", m_qcmp[i])], pool="qo")

    XPJ0, SZT0, SST0, YT0 = 0, 4096, 8192, 9216
    m_mm1 = {}
    m_evac = {}
    m_mm2 = {}
    m_btev = {}
    m_scan = {}
    m_mm3 = {}
    m_y = {}
    m_mm4 = {}
    m_sevac = {}
    m_sevb = {}
    m_sdma = {}
    prev_sdma = None
    qp_next = 0
    QTICK_FC = tuple(range(1, 16, 2))
    q_wq1 = {}

    def emit_wq1_reload(j):
        for c in range(NDC):
            q_wq1[c * 4 + j] = ld(
                BW1[:, c * F + j * 1024: c * F + (j + 1) * 1024],
                d_wq1[(c * 4 + j) * P:(c * 4 + j + 1) * P, :],
                waits=[("pe", m_mm1[(NG - 1, 8 * j + 7)]), m_qst[c * 4 + j]])

    def nfc_of(g):
        return NEC if g == 0 else NFC

    for g in range(NG):
        gc = g * GRP
        nfc_g = nfc_of(g)
        for fc in range(nfc_g):
            bank = PS[fc % 4]

            def mm1(e, fc=fc, gc=gc, bank=bank):
                last = None
                for c in range(NDC):
                    last = e.matmul(bank[:, 0:GRP],
                                    BW1[:, c * F + fc * P: c * F + (fc + 1) * P],
                                    XNT[:, c * TTOK + gc: c * TTOK + gc + GRP],
                                    start=(c == 0), stop=(c == NDC - 1))
                return last
            if fc >= 4:
                prev_ev = m_evac.get((g, fc - 4))
            elif g >= 1:
                prev_ev = m_evac.get((g - 1, nfc_of(g - 1) - 4 + fc))
            else:
                prev_ev = None
            w = [q_win[c * 4 + fc // 8] for c in range(NDC)] + \
                [("act", prev_ev),
                 ("dve", m_sevb.get((g - 1, fc)) if fc < 4 else None),
                 ("dve", m_trev_d.get(2 * g + 1)),
                 ("act", m_trev_s.get(2 * g + 1)),
                 ("dve", m_trev_d[NT - 1]) if g == 0 and fc < 4 else None,
                 ("act", m_trev_s[NT - 1]) if g == 0 and fc < 4 else None]
            m_mm1[(g, fc)] = pl.op("tensor", mm1, waits=w, inc="pe")

            tgt = XPJ0 + fc * GRP if fc < NEC else SZT0 + (fc - NEC) * GRP

            def ev(e, fc=fc, tgt=tgt, bank=bank):
                return e.activation(out=W2u(tgt, GRP), in_=bank[:, 0:GRP], func=AF.Silu,
                                    bias=BIN[:, fc:fc + 1], scale=1.0)
            ew = [("pe", m_mm1[(g, fc)]), mk_bin]
            if g >= 1:
                if fc < NEC:
                    ew.append(("pe", m_mm4.get((g - 1, NEC - 1), m_mm2.get(g - 1))))
                else:
                    ew.append(("dve", m_y.get((g - 1, fc - NEC))))
            m_evac[(g, fc)] = pl.op("scalar", ev, waits=ew, inc="act")
            if 1 <= g <= 4 and fc % 2 == 0:
                emit_gamma_tick()
            if g >= 5 and fc in QTICK_FC and qp_next < n_qp:
                emit_quant_piece(qp_next)
                qp_next += 1
                if len(q_pending) > 2:
                    emit_quant_store()
            if g == NG - 1 and fc in (8, 16, 24):
                emit_wq1_reload(fc // 8 - 1)

        def mm2(e):
            last = None
            for ec in range(NEC):
                last = e.matmul(PS[4][0:S, 0:GRP], WBT[:, ec * S:(ec + 1) * S],
                                W2u(XPJ0 + ec * GRP, GRP),
                                start=(ec == 0), stop=(ec == NEC - 1))
            return last
        m_mm2[g] = pl.op("tensor", mm2,
                         waits=[("act", m_evac[(g, NEC - 1)]),
                                ("dve", m_wbt),
                                ("dve", m_trev_d[NT - 1]) if g == 0 else None,
                                ("act", m_trev_s[NT - 1]) if g == 0 else None,
                                ("dve", m_btev.get(g - 1))],
                         inc="pe")
        if g == 0:
            def btev(e, gc=gc):
                return e.tensor_scalar(out=BTb(gc, GRP), in0=PS[4][0:S, 0:GRP],
                                       scalar1=0.1, scalar2=MASK[:], op0=ALU.mult,
                                       op1=ALU.mult)
        else:
            def btev(e, gc=gc):
                return e.tensor_scalar(out=BTb(gc, GRP), in0=PS[4][0:S, 0:GRP],
                                       scalar1=0.1, scalar2=None, op0=ALU.mult)
        m_btev[g] = pl.op("vector", btev,
                          waits=[("pe", m_mm2[g]), q_mask], inc="dve")
        if g == 0:
            def scan(e, gc=gc):
                return e.tensor_tensor_scan(out=HSTb(gc, GRP), data0=DEC9[:, 0:GRP],
                                            data1=BTb(gc, GRP), initial=0.0,
                                            op0=ALU.mult, op1=ALU.add)
        else:
            def scan(e, gc=gc):
                return e.tensor_tensor_scan(out=HSTb(gc, GRP), data0=DEC9[:, 0:GRP],
                                            data1=BTb(gc, GRP), initial=HSTb(gc - 1, 1),
                                            op0=ALU.mult, op1=ALU.add)
        m_scan[g] = pl.op("vector", scan, waits=[("dve", m_btev[g])], inc="dve")

        if g >= 1:
            def emit_mm3(ec):
                bank = PS[5 + ec % 2]

                def mm3(e, gc=gc, ec=ec, bank=bank):
                    e.matmul(bank[:, 0:GRP], WCTb(ec), HSTb(gc, GRP),
                             start=True, stop=False)
                    return e.matmul(bank[:, 0:GRP], B(DGM0 + ec * P, P),
                                    W2u(XPJ0 + ec * GRP, GRP),
                                    start=False, stop=True)
                prev_y = m_y.get((g, ec - 2)) if ec >= 2 else m_y.get((g - 1, 14 + ec))
                m_mm3[(g, ec)] = pl.op("tensor", mm3,
                                       waits=[("dve", m_scan[g]), q_small,
                                              ("dve", prev_y), ("dve", m_dgm),
                                              ("dve", m_trev_d[NT - 1])
                                              if g == 1 and ec < 2 else None,
                                              ("act", m_trev_s[NT - 1])
                                              if g == 1 and ec < 2 else None,
                                              ("act", m_evac[(g, ec)])],
                                       inc="pe")
                m_y[(g, ec)] = pl.op(
                    "vector",
                    lambda e, ec=ec, bank=bank: e.tensor_tensor(
                        out=W2u(XPJ0 + ec * GRP, GRP), in0=bank[:, 0:GRP],
                        in1=W2u(SZT0 + ec * GRP, GRP), op=ALU.mult),
                    waits=[("pe", m_mm3[(g, ec)]), ("act", m_evac[(g, NEC + ec)])],
                    inc="dve")

            emit_mm3(0)
            emit_mm3(1)

            def mm4bias(e):
                last = None
                for bb in range(4):
                    n = bb % 2
                    last = e.matmul(PS[bb][:], ONESR[:],
                                    BOUTBF[0:1, n * 512:(n + 1) * 512],
                                    start=True, stop=False)
                return last
            m_mm4b = pl.op("tensor", mm4bias,
                           waits=[("act", m_evac[(g, NFC - 1)]), q_small,
                                  ("dve", m_ones2),
                                  ("dve", m_sevac.get((g - 1, 1)))],
                           inc="pe")
            for ec in range(NEC):
                def mm4ec(e, ec=ec):
                    last = None
                    for tt in range(2):
                        for n in range(2):
                            last = e.matmul(PS[tt * 2 + n][:],
                                            W2u(XPJ0 + ec * GRP + tt * P, P),
                                            BW2[:, ec * D + n * 512: ec * D + (n + 1) * 512],
                                            start=False, stop=(ec == NEC - 1))
                    return last
                m_mm4[(g, ec)] = pl.op("tensor", mm4ec,
                                       waits=[("dve", m_y[(g, ec)]),
                                              q_wout[ec]],
                                       inc="pe")
                if ec + 2 < NEC:
                    emit_mm3(ec + 2)
                if g >= 5 and ec % 2 == 0 and qp_next < n_qp:
                    emit_quant_piece(qp_next)
                    qp_next += 1
                    if len(q_pending) > 2:
                        emit_quant_store()
            for tt in range(2):
                sslot = (lambda c0, n: W2u(SST0 + c0, n)) if tt == 0 else \
                        (lambda c0, n: SCANB[:, 4608 + c0: 4608 + c0 + n])
                for n_ in range(2):
                    bb = tt * 2 + n_
                    m_sevb[(g, bb)] = pl.op(
                        "vector",
                        lambda e, n_=n_, bb=bb, sslot=sslot: e.tensor_copy(
                            out=sslot(n_ * 512, 512), in_=PS[bb][:]),
                        waits=[("pe", m_mm4[(g, NEC - 1)]),
                               m_sdma.get((g - 1, tt)) if n_ == 0 else None],
                        inc="dve")
                m_sevac[(g, tt)] = m_sevb[(g, tt * 2 + 1)]
                mt = (g - 1) * 2 + tt
                m_sdma[(g, tt)] = pl.dma(
                    "sync",
                    lambda e, mt=mt, sslot=sslot: e.dma_start(
                        out=d_sspill[mt * P:(mt + 1) * P, :], in_=sslot(0, 1024)),
                    waits=[("dve", m_sevac[(g, tt)])], pool="qo")
                prev_sdma = m_sdma[(g, tt)]

    while qp_next < n_qp:
        emit_quant_piece(qp_next)
        qp_next += 1
    while q_pending:
        emit_quant_store()

    m_mm4_last = m_mm4[(NG - 1, NEC - 1)]
    emit_wq1_reload(3)

    def w2_waits(c):
        w = [("pe", m_mm4[(NG - 1, c)] if c < NEC else m_mm4_last),
             m_qst[NFC + c]]
        if 16 <= c <= 18:
            w += [("act", m_gabs[n_gp - 1]),
                  m_qst[n_qp - 1], m_qst[n_qp - 2], m_qst[n_qp - 3]]
        elif c == 19:
            w.append(("dve", m_qcmp[n_qp - 1]))
        elif 20 <= c <= 28:
            w += [("dve", m_sevac[(NG - 1, 0)]), m_sdma[(NG - 1, 0)]]
        return w
    q_w2 = [pl.dma("sync",
                   lambda e, c=c: e.dma_start(out=BW2[:, c * D:(c + 1) * D],
                                              in_=d_wq2[c * P:(c + 1) * P, :]),
                   waits=w2_waits(c), pool="qx")
            for c in range(NHC)]


    m_bcast_done_box = [None]

    def emit_f1_prep():
        m_v2ev = pl.chain("vector", [
            lambda e: e.tensor_copy(out=VROW[0:1, 0:512], in_=PS[7][0:1, 0:512]),
            lambda e: e.tensor_copy(out=VROW[32:33, 512:1024], in_=PS[7][32:33, 0:512]),
        ], waits=[("pe", m_v2f[NHC - 1])])
        m_v2mv = ("qmv", pl.op("sync",
                               lambda e: e.dma_start(out=VROW[0:1, 512:1024],
                                                     in_=VROW[32:33, 512:1024]),
                               waits=[("dve", m_v2ev)], inc="qmv", amt=16))
        m_v2bca = pl.op("tensor",
                        lambda e: e.matmul(PS[7][:], ONESRF[:], VROW[0:1, 0:512],
                                           start=True, stop=True),
                        waits=[m_v2mv, ("dve", m_ones2)], inc="pe")
        m_v2eva = pl.op("vector", lambda e: e.tensor_copy(out=V2B[:, 0:512], in_=PS[7][:]),
                        waits=[("pe", m_v2bca), ("dve", m_qcmp[n_qp - 1])],
                        inc="dve")
        m_v2bcb = pl.op("tensor",
                        lambda e: e.matmul(PS[7][:], ONESRF[:], VROW[0:1, 512:1024],
                                           start=True, stop=True),
                        waits=[("dve", m_v2eva)], inc="pe")
        m_v2evb = pl.op("vector", lambda e: e.tensor_copy(out=V2B[:, 512:1024], in_=PS[7][:]),
                        waits=[("pe", m_v2bcb)], inc="dve")

        def cs2mm(e):
            last = None
            for c in range(NHC):
                e.matmul(PS[7][0:1, 0:512], ONESC[:], BW2[:, c * D:c * D + 512],
                         start=(c == 0), stop=(c == NHC - 1))
                last = e.matmul(PS[7][32:33, 0:512], ONESC[:],
                                BW2[:, c * D + 512:(c + 1) * D],
                                start=(c == 0), stop=(c == NHC - 1))
            return last
        m_cs2 = pl.op("tensor", cs2mm,
                      waits=[("dve", m_fold[NHC - 1]), ("dve", m_v2evb)], inc="pe")
        m_cs2ev = pl.chain("vector", [
            lambda e: e.tensor_copy(out=VROW[0:1, 0:512], in_=PS[7][0:1, 0:512]),
            lambda e: e.tensor_copy(out=VROW[32:33, 512:1024], in_=PS[7][32:33, 0:512]),
        ], waits=[("pe", m_cs2)])
        m_cs2mv = ("qmv", pl.op("sync",
                                lambda e: e.dma_start(out=VROW[0:1, 512:1024],
                                                      in_=VROW[32:33, 512:1024]),
                                waits=[("dve", m_cs2ev)], inc="qmv", amt=16))
        m_cs2bca = pl.op("tensor",
                         lambda e: e.matmul(PS[7][:], ONESRF[:], VROW[0:1, 0:512],
                                            start=True, stop=True),
                         waits=[("dve", m_cs2ev)], inc="pe")
        m_cs2eva = pl.op("vector", lambda e: e.tensor_copy(out=CS2B[:, 0:512], in_=PS[7][:]),
                         waits=[("pe", m_cs2bca), m_sdma[(NG - 1, 1)]],
                         inc="dve")
        m_cs2bcb = pl.op("tensor",
                         lambda e: e.matmul(PS[7][:], ONESRF[:], VROW[0:1, 512:1024],
                                            start=True, stop=True),
                         waits=[m_cs2mv, ("dve", m_cs2eva)], inc="pe")
        m_bcast_done_box[0] = pl.op("vector",
                                    lambda e: e.tensor_copy(out=CS2B[:, 512:1024],
                                                            in_=PS[7][:]),
                                    waits=[("pe", m_cs2bcb)], inc="dve")

    HGT0, HGEL0, RAWG0, SQD2 = 0, 4096, 5120, 6144
    m_mm5 = {}
    m_gelu = {}
    m_sq = {}
    m_trf = {}
    m_trfev = {}
    m_mm6 = {}
    m_rawev = {}
    m_rawdma = {}
    m_stats_red = {}
    PREV3 = {0: 6, 1: 7, 2: 5}

    def emit_mm5(m, j):
        tcol = HALO + m * P
        bank = PS[j % 3]
        prev_g = m_gelu.get((m, j - 3)) if j >= 3 else m_gelu.get((m - 1, PREV3[j]))

        def mm5(e, j=j, tcol=tcol, bank=bank):
            last = None
            for c in range(NDC):
                last = e.matmul(bank[:],
                                XNT[:, c * TTOK + tcol: c * TTOK + tcol + P],
                                BW1[:, c * F + j * 512: c * F + (j + 1) * 512],
                                start=(c == 0), stop=(c == NDC - 1))
            return last
        psw = ("dve", m_sevb[(NG - 1, j)]) if m == 0 and j < 3 else None
        m_mm5[(m, j)] = pl.op("tensor", mm5,
                              waits=[q_wq1[c * 4 + j // 2] for c in range(NDC)]
                              + [("act", prev_g), psw], inc="pe")

    def emit_gelu_sq(m, j):
        bank = PS[j % 3]
        prev_tr = m_trf.get((m, j - 2)) if j >= 2 else m_trf.get((m - 1, 6 + j))
        prev_sq = m_sq.get((m, j - 2)) if j >= 2 else m_sq.get((m - 1, 6 + j))

        def gl(e, j=j, bank=bank):
            return e.activation(out=B(HGEL0 + 512 * (j % 2), 512), in_=bank[:],
                                func=AF.Gelu, accum_out=GS[:, j:j + 1])
        m_gelu[(m, j)] = pl.op("scalar", gl,
                               waits=[("pe", m_mm5[(m, j)]), ("pe", prev_tr),
                                      ("dve", prev_sq),
                                      ("dve", m_stats_red.get(m - 1))],
                               inc="act")

        def sq(e, j=j):
            hg = B(HGEL0 + 512 * (j % 2), 512)
            return e.scalar_tensor_tensor(out=B(SQD2, 512), in0=hg, scalar=1.0,
                                          in1=hg, op0=ALU.mult, op1=ALU.mult,
                                          accum_out=QS[:, j:j + 1])
        prev_sq2 = m_sq.get((m, j - 1)) if j >= 1 else m_sq.get((m - 1, 7))
        m_sq[(m, j)] = pl.op("vector", sq,
                             waits=[("act", m_gelu[(m, j)]),
                                    ("dve", prev_sq2)], inc="dve")

    def emit_trf(m, j):
        prev_trev = m_trfev.get((m, j - 2)) if j >= 2 else m_trfev.get((m - 1, 6 + j))

        def trf(e, j=j):
            hg = B(HGEL0 + 512 * (j % 2), 512)
            bankt = PS[3 + j % 2]
            last = None
            for b in range(4):
                last = e.matmul(bankt[:, b * P:(b + 1) * P], hg[:, b * P:(b + 1) * P],
                                IDN[:], start=True, stop=True)
            return last
        m_trf[(m, j)] = pl.op("tensor", trf,
                              waits=[("act", m_gelu[(m, j)]), ("dve", prev_trev),
                                     ("dve", m_sevb[(NG - 1, 3)])
                                     if m == 0 and j < 2 else None],
                              inc="pe")
        m_trfev[(m, j)] = pl.op(
            "vector",
            lambda e, j=j: e.tensor_copy(out=B(HGT0 + j * 512, 512),
                                         in_=PS[3 + j % 2][:]),
            waits=[("pe", m_trf[(m, j)]), ("pe", m_mm6.get((m - 1, 1)))],
            inc="dve")

    m_v2f = {}
    m_fold = {}

    def emit_v2_fold(c):
        def v2mm(e, c=c):
            e.matmul(PS[7][0:1, 0:512], LNBbf[:, c:c + 1], BW2[:, c * D:c * D + 512],
                     start=(c == 0), stop=(c == NHC - 1))
            return e.matmul(PS[7][32:33, 0:512], LNBbf[:, c:c + 1],
                            BW2[:, c * D + 512:(c + 1) * D],
                            start=(c == 0), stop=(c == NHC - 1))
        m_v2f[c] = pl.op("tensor", v2mm,
                         waits=[q_w2[c], q_lnbbf], inc="pe")
        m_fold[c] = pl.op(
            "vector",
            lambda e, c=c: e.tensor_scalar(
                out=BW2[:, c * D:(c + 1) * D], in0=BW2[:, c * D:(c + 1) * D],
                scalar1=LNW[:, c:c + 1], scalar2=None, op0=ALU.mult),
            waits=[("pe", m_v2f[c]), mk_lnw], inc="dve")

    m_f2b_t = {}
    m_f3_sld = {}
    m_f3_gld = {}
    m_f3_xld = {}
    m_f3_o1 = {}
    m_f3_tanh = {}
    m_f3_fin = {}
    m_f3_odma = {}
    f3_next = [0]
    pending_tanh = []

    def emit_f2_cols(c0, c1):
        s4 = slice(c0, c1)
        mu, sq = MUT[:, s4], SQT[:, s4]
        vet, nta = VET[:, s4], NTA[:, s4]
        rst, nrm = RST[:, s4], NRM[:, s4]
        a = pl.chain("vector", [
            lambda e: e.tensor_scalar(out=mu, in0=mu, scalar1=1.0 / HH,
                                      scalar2=None, op0=ALU.mult),
            lambda e: e.tensor_scalar(out=sq, in0=sq, scalar1=1.0 / HH,
                                      scalar2=None, op0=ALU.mult),
            lambda e: e.tensor_tensor(out=nta, in0=mu, in1=mu, op=ALU.mult),
            lambda e: e.tensor_tensor(out=vet, in0=sq, in1=nta, op=ALU.subtract),
            lambda e: e.tensor_scalar(out=vet, in0=vet, scalar1=1e-5,
                                      scalar2=None, op0=ALU.add),
        ], waits=[("dve", m_stats_red[c1 - 1])])
        sd = pl.op("scalar", lambda e: e.activation(out=nta, in_=vet, func=AF.Sqrt),
                   waits=[("dve", a)], inc="act")
        mk = pl.chain("vector", [
            lambda e: e.reciprocal(out=rst, in_=nta),
            lambda e: e.tensor_tensor(out=nta, in0=rst, in1=rst, op=ALU.mult),
            lambda e: e.tensor_tensor(out=nta, in0=nta, in1=vet, op=ALU.mult),
            lambda e: e.tensor_scalar(out=nta, in0=nta, scalar1=-0.5, scalar2=1.5,
                                      op0=ALU.mult, op1=ALU.add),
            lambda e: e.tensor_tensor(out=rst, in0=rst, in1=nta, op=ALU.mult),
            lambda e: e.tensor_tensor(out=nta, in0=mu, in1=rst, op=ALU.mult),
            lambda e: e.tensor_scalar(out=nrm, in0=nta, scalar1=-1.0,
                                      scalar2=None, op0=ALU.mult),
        ], waits=[("act", sd)])
        for t in range(c0, c1):
            m_f2b_t[t] = mk

    def f3_sbuf(t):
        if t % 2 == 0:
            return F3S[:, 0:1024], F3S[:, 1024:2048]
        return SCANB[:, 2048:3072], SCANB[:, 3072:4096]

    def emit_f3_loads(t):
        if t in m_f3_sld or t >= NMT:
            return
        sre_t, gre_t = f3_sbuf(t)
        bw = ("act", m_sqln[16 + t]) if t < 2 else None
        bwd = ("dve", m_xnb[16 + t]) if t < 2 else None
        bw2 = ("dve", m_ones2) if t == 0 else None
        m_f3_sld[t] = pl.dma(
            "sync",
            lambda e, t=t, d=sre_t: e.dma_start(out=d, in_=d_sspill[t * P:(t + 1) * P, :]),
            waits=[m_sdma[(t // 2 + 1, t % 2)], bw, bwd, bw2,
                   ("dve", m_f3_fin.get(2 * (t - 2) + 1))], pool="qx")
        m_f3_gld[t] = pl.dma(
            "sync",
            lambda e, t=t, d=gre_t: e.dma_start(out=d, in_=d_gspill[t * P:(t + 1) * P, :]),
            waits=[m_rawdma[t], bw, bwd,
                   ("dve", m_f3_o1.get(2 * (t - 2) + 1))], pool="qx")

    def emit_f3_xld(t):
        if t in m_f3_xld or t >= NMT:
            return
        m_f3_xld[t] = pl.dma(
            "sync",
            lambda e, t=t: e.dma_start(out=F3X[:, 0:1024],
                                       in_=d_x[(t + 2) * P:(t + 3) * P, :]),
            waits=[("dve", m_f3_fin.get(2 * (t - 1) + 1)),
                   ("dve", m_thr[0]) if t == 0 else None,
                   ("act", m_gabs[n_gp - 1]) if t == 0 else None,
                   ("dve", m_ones2) if t == 0 else None], pool="qx")

    def emit_f3_o1(i):
        m, hv = i // 2, i % 2
        if hv == 0:
            emit_f3_loads(m)
            emit_f3_loads(m + 1)
            emit_f3_xld(m)
        sa = Fv(512 * (i % 2), 512)
        sb2 = Fv(512 * ((i + 1) % 2), 512)
        sre_t, gre_t = f3_sbuf(m)
        greh = gre_t[:, hv * 512:(hv + 1) * 512]
        m_f3_o1[i] = pl.chain("vector", [
            lambda e, m=m, hv=hv, sa=sa: e.scalar_tensor_tensor(
                out=sa, in0=CS2B[:, hv * 512:(hv + 1) * 512],
                scalar=NRM[:, m:m + 1], in1=V2B[:, hv * 512:(hv + 1) * 512],
                op0=ALU.mult, op1=ALU.add),
            lambda e, m=m, greh=greh, sa=sa: e.scalar_tensor_tensor(
                out=sa, in0=greh, scalar=RST[:, m:m + 1], in1=sa,
                op0=ALU.mult, op1=ALU.add),
        ], waits=[("dve", m_f2b_t[m]), m_f3_gld[m],
                  ("dve", m_bcast_done_box[0])])
        pending_tanh.append(i)

    def emit_f3_tanh(i):
        sa = Fv(512 * (i % 2), 512)
        sb2 = Fv(512 * ((i + 1) % 2), 512)
        m_f3_tanh[i] = pl.op(
            "scalar",
            lambda e, sa=sa, sb2=sb2: e.activation(out=sb2, in_=sa, func=AF.Tanh,
                                                   scale=0.5),
            waits=[("dve", m_f3_o1[i])], inc="act")

    def emit_f3_fin(i):
        m, hv = i // 2, i % 2
        sa = Fv(512 * (i % 2), 512)
        sb2 = Fv(512 * ((i + 1) % 2), 512)
        ob = Fv(1024 + hv * 512, 512)
        sre_t, _ = f3_sbuf(m)
        sreh = sre_t[:, hv * 512:(hv + 1) * 512]
        xreh = F3X[:, hv * 512:(hv + 1) * 512]
        m_f3_fin[i] = pl.chain("vector", [
            lambda e, sa=sa, sb2=sb2: e.scalar_tensor_tensor(
                out=sa, in0=sb2, scalar=1.0, in1=sa, op0=ALU.add, op1=ALU.mult),
            lambda e, sa=sa, sb2=sb2, sreh=sreh: e.scalar_tensor_tensor(
                out=sb2, in0=sa, scalar=0.5, in1=sreh, op0=ALU.mult, op1=ALU.mult),
            lambda e, sb2=sb2, ob=ob, xreh=xreh: e.tensor_tensor(
                out=ob, in0=sb2, in1=xreh, op=ALU.add),
        ], waits=[("act", m_f3_tanh[i]), m_f3_sld[m], m_f3_xld[m],
                  m_f3_odma.get(m - 1)])
        if hv == 1:
            m_f3_odma[m] = pl.dma(
                "sync",
                lambda e, m=m: e.dma_start(out=d_out[m * P:(m + 1) * P, :],
                                           in_=Fv(1024, 1024)),
                waits=[("dve", m_f3_fin[i])], pool="qo")

    for m in range(NMT):
        if m == 1:
            emit_f1_prep()
        emit_mm5(m, 0)
        for j in range(8):
            if m == 0:
                for c in range(4 * j, 4 * j + 4):
                    emit_v2_fold(c)
            emit_gelu_sq(m, j)
            if j < 7:
                emit_mm5(m, j + 1)
            emit_trf(m, j)
            if j in (2, 3, 4) and pending_tanh:
                emit_f3_tanh(pending_tanh.pop(0))
        m_stats_red[m] = pl.chain("vector", [
            lambda e, m=m: e.reduce_sum(out=MUT[:, m:m + 1], in_=GS, axis=AX.X),
            lambda e, m=m: e.reduce_sum(out=SQT[:, m:m + 1], in_=QS, axis=AX.X),
        ], waits=[("act", m_gelu[(m, 7)]), ("dve", m_sq[(m, 7)])])
        f3_allowed = 2 * (m - 1) if m >= 13 else (m // 4) * 8
        for _ in range(3):
            if f3_next[0] < f3_allowed:
                i_ = f3_next[0]
                f3_next[0] += 1
                if i_ >= 1:
                    while pending_tanh and pending_tanh[0] <= i_ - 1:
                        emit_f3_tanh(pending_tanh.pop(0))
                    emit_f3_fin(i_ - 1)
                emit_f3_o1(i_)
        for n in range(2):
            def mm6(e, n=n):
                bank6 = PS[5 + n]
                last = None
                for fc in range(NHC):
                    last = e.matmul(bank6[:], B(HGT0 + fc * P, P),
                                    BW2[:, fc * D + n * 512: fc * D + (n + 1) * 512],
                                    start=(fc == 0), stop=(fc == NHC - 1))
                return last
            m_mm6[(m, n)] = pl.op("tensor", mm6,
                                  waits=[("dve", m_trfev[(m, 7)]),
                                         ("dve", m_fold[NHC - 1]),
                                         ("dve", m_rawev.get((m - 1, n)))],
                                  inc="pe")
            rawb = (B(RAWG0, 1024) if m % 2 == 0
                    else SCANB[:, 4608:5632])
            m_rawev[(m, n)] = pl.op(
                "vector",
                lambda e, n=n, rawb=rawb: e.tensor_copy(
                    out=rawb[:, n * 512:(n + 1) * 512], in_=PS[5 + n][:]),
                waits=[("pe", m_mm6[(m, n)]), m_rawdma.get(m - 2),
                       m_sdma[(NG - 1, 1)] if (m, n) == (1, 0) else None],
                inc="dve")
        m_rawdma[m] = pl.dma(
            "sync",
            lambda e, m=m, rawb=rawb: e.dma_start(
                out=d_gspill[m * P:(m + 1) * P, :], in_=rawb),
            waits=[("dve", m_rawev[(m, 1)])], pool="qo")
        if m % 4 == 3 and m < 12:
            emit_f2_cols(m - 3, m + 1)
        elif m in (13, 15):
            emit_f2_cols(m - 1, m + 1)

    while f3_next[0] < 2 * NMT:
        i_ = f3_next[0]
        f3_next[0] += 1
        if i_ >= 1:
            while pending_tanh and pending_tanh[0] <= i_ - 1:
                emit_f3_tanh(pending_tanh.pop(0))
            emit_f3_fin(i_ - 1)
        emit_f3_o1(i_)
    while pending_tanh:
        emit_f3_tanh(pending_tanh.pop(0))
    emit_f3_fin(2 * NMT - 1)

    if _DEBUG:
        fa = pl.snap()

        def dmp(name, src):
            pl.dma("sync", lambda e, d=dbg[name], s=src: e.dma_start(out=d, in_=s),
                   waits=[("dve", fa["dve"]), ("act", fa["act"]), ("pe", fa["pe"]),
                          ("pool", fa["pool"])],
                   pool="qo")
        dmp("xnt", XNT[:])
        dmp("bt", BTb(0, TTOK))
        dmp("hst", HSTb(0, TTOK))
        dmp("thr", THR[:])
        dmp("v2b", V2B[:])
        dmp("cs2b", CS2B[:])
        dmp("mut", MUT)
        dmp("rst", RST)

    qo_waits = [(s, c) for s, c in pl.snap().items()
                if (s.startswith("qo") or s == "qsm") and c > 0]
    pl.op("sync", None, waits=qo_waits)
    endm = pl.snap()
    pl.barrier(endm)

    with nc.Block() as blk:
        def runner(name):
            stream = pl.stream[name]

            def go(eng):
                for waits, fn, inc in stream:
                    for s_, v in waits:
                        eng.wait_ge(sems[s_], v)
                    if fn is None:
                        continue
                    if isinstance(fn, tuple) and fn[0] == "chain":
                        _, fns, base, csem = fn
                        for j, f in enumerate(fns):
                            f(eng).then_inc(sems[csem], 1)
                            eng.wait_ge(sems[csem], base + j + 1)
                        continue
                    inst = fn(eng)
                    if inc is not None:
                        s_, amt = inc
                        inst.then_inc(sems[s_], amt)
            return go
        blk.sync(runner("sync"))
        blk.gpsimd(runner("gpsimd"))
        blk.tensor(runner("tensor"))
        blk.vector(runner("vector"))
        blk.scalar(runner("scalar"))

    es.close()
    return nc


def _prep_inputs(L0_attn_x, L0_attn_W_in, L0_attn_b_in, L0_attn_W_B, L0_attn_W_C,
                 L0_attn_D, L0_attn_W_out, L0_attn_b_out, L0_ffn_W_fc, L0_ffn_ln_w,
                 L0_ffn_ln_b, L0_ffn_W_proj):
    f32 = np.float32
    x = np.ascontiguousarray(np.asarray(L0_attn_x, f32).reshape(16384, 1024))
    common = dict(
        w_in_t=np.ascontiguousarray(
            np.asarray(L0_attn_W_in, f32).T.reshape(8, 128, 4, 1024)
            .transpose(0, 2, 1, 3).reshape(F, D)),
        w_out_t=np.ascontiguousarray(np.asarray(L0_attn_W_out, f32).T),
        w_fc_t=np.ascontiguousarray(
            np.asarray(L0_ffn_W_fc, f32).T.reshape(8, 128, 4, 1024)
            .transpose(0, 2, 1, 3).reshape(HH, D)),
        w_proj_t=np.ascontiguousarray(np.asarray(L0_ffn_W_proj, f32).T),
        w_b_t=np.ascontiguousarray(
            np.asarray(L0_attn_W_B, f32).T.reshape(NEC, P, S)
            .transpose(1, 0, 2).reshape(P, NEC * S)),
        w_c_t=np.ascontiguousarray(np.asarray(L0_attn_W_C, f32).T),
        par_fm=np.ascontiguousarray(np.concatenate([
            np.asarray(L0_attn_b_in, f32).reshape(NFC, P).T,
            np.asarray(L0_attn_D, f32).reshape(NEC, P).T,
            np.asarray(L0_ffn_ln_w, f32).reshape(NHC, P).T,
        ], axis=1)),
        b_out_row=np.ascontiguousarray(np.asarray(L0_attn_b_out, f32).reshape(1, D)),
        lnb_fm=np.ascontiguousarray(np.asarray(L0_ffn_ln_b, f32).reshape(NHC, P).T),
        ident=np.eye(P, dtype=f32).astype(ml_dtypes.bfloat16),
    )
    in_maps = []
    for c in range(8):
        lo = c * TC
        if c == 0:
            xs = np.concatenate([np.zeros((HALO, D), f32), x[0:TC]], axis=0)
            mask = np.zeros((S, 1), f32)
        else:
            xs = x[lo - HALO: lo + TC]
            mask = np.ones((S, 1), f32)
        m = dict(common)
        m["xin"] = np.ascontiguousarray(xs)
        m["mask16"] = mask
        in_maps.append(m)
    return in_maps


def kernel(**inputs):
    in_maps = _prep_inputs(**inputs)
    nc = build_nc()
    res = run_bass_kernel_spmd(nc, in_maps, core_ids=list(range(8)))
    out = np.concatenate([np.asarray(r["out"], np.float32) for r in res.results], axis=0)
    return out.reshape(1, 16384, 1024)
















# version 64 (final)
# speedup vs baseline: 1.0263x; 1.0263x over previous
import numpy as np
import ml_dtypes
from contextlib import ExitStack

import concourse.bass as bass
import concourse.mybir as mybir
from concourse.bass_utils import run_bass_kernel_spmd

F32 = mybir.dt.float32
BF16 = mybir.dt.bfloat16
AF = mybir.ActivationFunctionType
ALU = mybir.AluOpType
AX = mybir.AxisListType

P = 128
D = 1024
F = 4096
E = 2048
S = 16
HH = 4096
TC = 2048
HALO = 256
TTOK = TC + HALO
NDC, NFC, NEC, NHC = 8, 32, 16, 32
NT = TTOK // P
GRP = 256
NG = TTOK // GRP
NMT = TC // P
NELEM_W = 4096 * 1024

ENGS = ("sync", "gpsimd", "tensor", "vector", "scalar")
_DEBUG = False


class Plan:
    DMA_SLOTS = 10

    def __init__(self):
        self.stream = {e: [] for e in ENGS}
        self.count = {}
        self.last_wait = {e: {} for e in ENGS}
        self.dma_idx = {}

    def sem(self, name):
        self.count[name] = 0

    def op(self, eng, fn, waits=(), inc=None, amt=1):
        w = []
        for wv in waits:
            if wv is None:
                continue
            s, v = wv
            if v is None or v <= 0:
                continue
            if self.last_wait[eng].get(s, -1) >= v:
                continue
            self.last_wait[eng][s] = v
            w.append((s, v))
        self.stream[eng].append((w, fn, (inc, amt) if inc else None))
        if inc is not None:
            self.count[inc] += amt
            return self.count[inc]
        return None

    def chain(self, eng, fns, waits=(), inc="dve"):
        w = []
        for wv in waits:
            if wv is None:
                continue
            s, v = wv
            if v is None or v <= 0:
                continue
            if self.last_wait[eng].get(s, -1) >= v:
                continue
            self.last_wait[eng][s] = v
            w.append((s, v))
        base = self.count[inc]
        self.count[inc] += len(fns)
        self.last_wait[eng][inc] = self.count[inc]
        self.stream[eng].append((w, ("chain", fns, base, inc), None))
        return self.count[inc]

    def dma(self, eng, fn, waits=(), pool="qx"):
        k = self.dma_idx.get(pool, 0)
        self.dma_idx[pool] = k + 1
        sem = f"{pool}{k % self.DMA_SLOTS}"
        w = list(waits)
        if k >= self.DMA_SLOTS:
            w.append((sem, 16 * (k // self.DMA_SLOTS)))
        cnt = self.op(eng, fn, w, inc=sem, amt=16)
        return (sem, cnt)

    def snap(self):
        return dict(self.count)

    def barrier(self, marks):
        for e in ENGS:
            w = []
            for s, v in marks.items():
                if v and self.last_wait[e].get(s, -1) < v:
                    self.last_wait[e][s] = v
                    w.append((s, v))
            if w:
                self.stream[e].append((w, None, None))


def build_nc():
    nc = bass.Bass("TRN2", target_bir_lowering=False)

    d_x = nc.dram_tensor("xin", [TTOK, D], F32, kind="ExternalInput").ap()
    d_wint = nc.dram_tensor("w_in_t", [F, D], F32, kind="ExternalInput").ap()
    d_woutt = nc.dram_tensor("w_out_t", [E, D], F32, kind="ExternalInput").ap()
    d_wfct = nc.dram_tensor("w_fc_t", [HH, D], F32, kind="ExternalInput").ap()
    d_wpjt = nc.dram_tensor("w_proj_t", [HH, D], F32, kind="ExternalInput").ap()
    d_wbt = nc.dram_tensor("w_b_t", [P, NEC * S], F32, kind="ExternalInput").ap()
    d_wct = nc.dram_tensor("w_c_t", [S, E], F32, kind="ExternalInput").ap()
    d_par = nc.dram_tensor("par_fm", [P, NFC + NEC + NHC], F32,
                           kind="ExternalInput").ap()
    d_bout = nc.dram_tensor("b_out_row", [1, D], F32, kind="ExternalInput").ap()
    d_lnb = nc.dram_tensor("lnb_fm", [P, NHC], F32, kind="ExternalInput").ap()
    d_mask = nc.dram_tensor("mask16", [S, 1], F32, kind="ExternalInput").ap()
    d_idn = nc.dram_tensor("ident", [P, P], BF16, kind="ExternalInput").ap()
    d_out = nc.dram_tensor("out", [TC, D], F32, kind="ExternalOutput").ap()
    d_sspill = nc.dram_tensor("s_spill", [TC, D], BF16, kind="Internal").ap()
    d_gspill = nc.dram_tensor("g_spill", [TC, D], BF16, kind="Internal").ap()
    d_wq1 = nc.dram_tensor("wq1_img", [NDC * 4 * P, 1024], BF16, kind="Internal").ap()
    d_wq2 = nc.dram_tensor("wq2_img", [NHC * P, 1024], BF16, kind="Internal").ap()

    dbg = {}
    if _DEBUG:
        for nm, shape, dt in [
            ("xnt", [P, NDC * TTOK], BF16), ("bt", [S, TTOK], BF16),
            ("hst", [S, TTOK], BF16), ("thr", [P, 4], F32),
            ("v2b", [P, D], BF16), ("cs2b", [P, D], BF16),
            ("mut", [P, NMT], F32), ("rst", [P, NMT], F32),
        ]:
            dbg[nm] = nc.dram_tensor("dbg_" + nm, shape, dt, kind="ExternalOutput").ap()

    es = ExitStack()

    def sb(name, shape, dt):
        return es.enter_context(nc.sbuf_tensor(name, shape, dt))

    XNT = sb("XNT", [P, NDC * TTOK], BF16)
    BW1 = sb("BW1", [P, NDC * F], BF16)
    BW2 = sb("BW2", [P, 32768], BF16)
    SCRB = sb("SCRB", [P, 6656], BF16)
    SCRF = sb("SCRF", [P, 2048], F32)
    SCANB = sb("SCANB", [P, 5632], BF16)
    F3S = sb("F3S", [P, 2048], BF16)
    F3X = sb("F3X", [P, 1024], F32)
    F3SF = F3S.bitcast(F32)
    SCANBF = SCANB.bitcast(F32)
    WBT = F3S[:, 0:NEC * S]
    DEC9 = sb("DEC9", [S, GRP], F32)
    PAR = sb("PAR", [P, NFC + NEC + NHC], F32)
    BIN = PAR[:, 0:NFC]
    DFM = PAR[:, NFC:NFC + NEC]
    LNW = PAR[:, NFC + NEC:NFC + NEC + NHC]
    LNBbf = sb("LNBbf", [P, NHC], BF16)
    MASK = sb("MASK", [S, 1], F32)
    IDN = sb("IDN", [P, P], BF16)
    ONESR = F3S[0:1, 256:256 + P]
    ONESRF = F3X[0:1, 0:P]
    ONESC = sb("ONESC", [P, 1], BF16)
    ONESCF = sb("ONESCF", [P, 1], F32)
    GACC = F3X[:, 138:140]
    THRROW = F3X[0:1, 128:132]
    THRTMP = F3X[0:1, 136:138]
    THR = F3X[:, 132:136]
    GSQ = sb("GSQ", [P, 16], F32)
    GS, QS = GSQ[:, 0:8], GSQ[:, 8:16]
    STF = sb("STF", [P, 6 * NMT], F32)
    GPARTS = F3X[:, 144:208]
    MUT, SQT, VET = STF[:, 0:16], STF[:, 16:32], STF[:, 32:48]
    RST, NRM, NTA = STF[:, 48:64], STF[:, 64:80], STF[:, 80:96]

    PS = [es.enter_context(nc.psum_tensor(f"pb{i}", [P, 512], F32)) for i in range(8)]

    def B(c0, n):
        return SCRB[:, c0:c0 + n]

    def Fv(c0, n):
        return SCRF[:, c0:c0 + n]

    def BTb(c0, n):
        return SCANB[0:S, c0:c0 + n]

    def HSTb(c0, n):
        return SCANB[0:S, 2304 + c0: 2304 + c0 + n]

    def WCTb(ec):
        return BW2[0:S, 30720 + ec * P: 30720 + (ec + 1) * P]

    def W2u(c0, n):
        return BW2[:, 20480 + c0: 20480 + c0 + n]

    BOUTBF = BW2[0:1, 29696:30720]
    VROW = SCRF[0:64, 0:1024]
    V2B = SCANB[:, 0:1024]
    CS2B = SCANB[:, 1024:2048]

    sems = {}
    semnames = ["pe", "dve", "act", "pool", "qmask", "qsm", "qmv"]
    for pool_ in ("qx", "qw", "qo"):
        for k in range(Plan.DMA_SLOTS):
            semnames.append(f"{pool_}{k}")
    for s_ in semnames:
        sems[s_] = es.enter_context(nc.semaphore(s_))

    pl = Plan()
    for s_ in sems:
        pl.sem(s_)

    def ld(dst, src, waits=()):
        return pl.dma("sync", lambda e, d=dst, s=src: e.dma_start(out=d, in_=s),
                      waits=waits, pool="qx")

    mk_par = ld(PAR[:], d_par)
    mk_bin = mk_dfm = mk_lnw = mk_par
    mk_idn = ld(IDN[:], d_idn)
    pl.op("sync", lambda e: e.dma_start(out=MASK[:], in_=d_mask),
          inc="qmask", amt=16)
    q_params = [mk_par, mk_idn]

    def cld(dst, src):
        return pl.dma("gpsimd", lambda e, d=dst, s=src: e.dma_start(out=d, in_=s),
                      pool="qw")

    q_lnbbf = cld(LNBbf[:], d_lnb)
    pl.op("gpsimd", lambda e: e.dma_start(out=BOUTBF, in_=d_bout), inc="qsm", amt=16)
    pl.op("gpsimd", lambda e: e.dma_start(out=BW2[0:S, 30720:32768], in_=d_wct),
          inc="qsm", amt=16)
    q_small = ("qsm", 32)
    q_mask = ("qmask", 16)
    q_win = {}
    q_wout = {}

    d_memset = pl.chain("vector", [
        (lambda e, a=ap_, vv=v_: e.memset(a, vv))
        for ap_, v_ in ((DEC9[:], 0.9), (ONESC[:], 1.0), (ONESCF[:], 1.0))
    ])

    gpieces = []
    for c in range(NDC):
        for j in range(4):
            pk = c * 4 + j
            gpieces.append((0, d_wfct[pk * P:(pk + 1) * P, :]))
    for c in range(NHC):
        gpieces.append((1, d_wpjt[c * P:(c + 1) * P, :]))
    n_gp = len(gpieces)
    MQ = 16384
    m_gin = {}
    m_gabs = {}
    ga_slot = {}

    def emit_gamma_load(i):
        if i >= n_gp:
            return
        w_, src_ = gpieces[i]
        k, buf, kwaits = xf_alloc()
        ga_slot[i] = (k, buf)
        m_gin[i] = pl.dma("sync", lambda e, d=buf, s=src_: e.dma_start(out=d, in_=s),
                          waits=kwaits, pool="qx")

    def emit_gamma_abs(i):
        k, buf = ga_slot[i]
        m_gabs[i] = pl.op(
            "scalar",
            lambda e, d=buf, i=i: e.activation(out=BW2[:, MQ:MQ + 1024], in_=d,
                                               func=AF.Abs,
                                               accum_out=GPARTS[:, i:i + 1]),
            waits=[m_gin[i], ("act", m_gabs.get(i - 1)),
                   ("act", m_sqln[NT - 1]) if i == 0 else None,
                   ("dve", m_xnb[NT - 1]) if i == 0 else None], inc="act")
        xf_cons[k] = [("act", m_gabs[i])]

    def emit_gamma_tick():
        j = ga_state["j"]
        if j >= n_gp:
            return
        emit_gamma_abs(j)
        emit_gamma_load(j + 1)
        ga_state["j"] = j + 1

    ga_state = {"j": 0}

    xf_state = {"k": 0}
    xf_cons = {}

    def xf_alloc():
        k = xf_state["k"]
        xf_state["k"] += 1
        assert k < 2 or (k - 2) in xf_cons, f"xf slot {k} missing consumer"
        return k, Fv(1024 * (k % 2), 1024), list(xf_cons.get(k - 2, ()))

    SXL, SQL = STF[:, 0:NT], STF[:, 18:18 + NT]
    NMU, RSL = STF[:, 36:36 + NT], STF[:, 54:54 + NT]
    TMPL = STF[:, 72:72 + NT]
    m_xload = {}
    m_red = {}
    m_sqln = {}
    m_xnb = {}
    m_l2b = {}
    m_trev_d = {}
    m_trev_s = {}
    m_trpe = {}
    XL1 = [F3X[:, 0:1024], F3SF[:, 0:1024], SCANBF[:, 1024:2048]]
    TRB = [(PS[0], PS[1]), (PS[2], PS[3]), (PS[4], PS[5])]
    m_l2a = {}

    m_cast = {}

    def emit_ln_part1(t):
        xt = XL1[t % 3]
        m_xload[t] = pl.dma(
            "sync", lambda e, d=xt, s=d_x[t * P:(t + 1) * P, :]: e.dma_start(out=d, in_=s),
            waits=[("dve", m_cast.get(t - 3)), ("act", m_sqln.get(t - 3))],
            pool="qx")
        m_red[t] = pl.op(
            "scalar",
            lambda e, x=xt, t=t: e.activation(out=BW2[:, MQ:MQ + 1024], in_=x,
                                              func=AF.Copy, accum_out=SXL[:, t:t + 1]),
            waits=[m_xload[t], ("act", m_sqln.get(t - 1))], inc="act")
        m_sqln[t] = pl.op(
            "scalar",
            lambda e, x=xt, t=t: e.activation(out=BW2[:, MQ:MQ + 1024], in_=x,
                                              func=AF.Square, accum_out=SQL[:, t:t + 1]),
            waits=[("act", m_red[t])], inc="act")
        sx, sq = SXL[:, t:t + 1], SQL[:, t:t + 1]
        tm = TMPL[:, t:t + 1]
        m_l2a[t] = pl.chain("vector", [
            lambda e, sx=sx: e.tensor_scalar(out=sx, in0=sx, scalar1=1.0 / D,
                                             scalar2=None, op0=ALU.mult),
            lambda e, sq=sq: e.tensor_scalar(out=sq, in0=sq, scalar1=1.0 / D,
                                             scalar2=None, op0=ALU.mult),
            lambda e, sx=sx, tm=tm: e.tensor_tensor(out=tm, in0=sx, in1=sx,
                                                    op=ALU.mult),
            lambda e, sq=sq, tm=tm: e.tensor_tensor(out=sq, in0=sq, in1=tm,
                                                    op=ALU.subtract),
            lambda e, sq=sq: e.tensor_scalar(out=sq, in0=sq, scalar1=1e-5,
                                             scalar2=None, op0=ALU.add),
        ], waits=[("act", m_sqln[t])])
        m_cast[t] = pl.op(
            "vector",
            lambda e, x=xt, t=t: e.tensor_copy(out=B(1024 * (t % 4), 1024), in_=x),
            waits=[m_xload[t], ("pe", m_trpe.get(t - 4))], inc="dve")

    def emit_ln_part2(t):
        sx, sq = SXL[:, t:t + 1], SQL[:, t:t + 1]
        tm = TMPL[:, t:t + 1]
        nm, rs = NMU[:, t:t + 1], RSL[:, t:t + 1]
        sd = pl.op("scalar",
                   lambda e, sq=sq, tm=tm: e.activation(out=tm, in_=sq, func=AF.Sqrt),
                   waits=[("dve", m_l2a[t])], inc="act")
        m_l2b[t] = pl.chain("vector", [
            lambda e, rs=rs, tm=tm: e.reciprocal(out=rs, in_=tm),
            lambda e, sx=sx, nm=nm: e.tensor_scalar(out=nm, in0=sx, scalar1=-1.0,
                                                    scalar2=None, op0=ALU.mult),
        ], waits=[("act", sd)])
        xnb = B(1024 * (t % 4), 1024)
        m_xnb[t] = pl.op(
            "vector",
            lambda e, o=xnb, t=t: e.tensor_scalar(
                out=o, in0=o, scalar1=NMU[:, t:t + 1], scalar2=RSL[:, t:t + 1],
                op0=ALU.add, op1=ALU.mult),
            waits=[("dve", m_l2b[t])], inc="dve")
        banks = TRB[t % 3]

        def tr(e, xnb=xnb, banks=banks):
            last = None
            for c in range(NDC):
                bank = banks[c // 4]
                off = (c % 4) * P
                last = e.matmul(bank[:, off:off + P],
                                xnb[:, c * P:(c + 1) * P], IDN[:],
                                start=True, stop=True)
            return last
        m_trpe[t] = pl.op("tensor", tr,
                          waits=[("dve", m_xnb[t]), mk_idn,
                                 ("dve", m_trev_d.get(t - 3)),
                                 ("act", m_trev_s.get(t - 3))], inc="pe")

    def emit_ln_part3(t):
        banks = TRB[t % 3]

        def xv(bi, t=t, banks=banks):
            return XNT[:].rearrange("p (c tt) -> p c tt", c=NDC)[
                :, bi * 4:(bi + 1) * 4, t * P:(t + 1) * P]
        m_trev_d[t] = pl.op(
            "vector",
            lambda e, t=t, banks=banks: e.tensor_copy(
                out=xv(0), in_=banks[0][:].rearrange("p (c f) -> p c f", c=4)),
            waits=[("pe", m_trpe[t])], inc="dve")
        m_trev_s[t] = pl.op(
            "scalar",
            lambda e, t=t, banks=banks: e.activation(
                out=xv(1), in_=banks[1][:].rearrange("p (c f) -> p c f", c=4),
                func=AF.Copy),
            waits=[("pe", m_trpe[t])], inc="act")

    for t in range(NT):
        if t >= 2:
            emit_ln_part2(t - 2)
        if t >= 3:
            emit_ln_part3(t - 3)
        emit_ln_part1(t)
    for t_ in (NT - 2, NT - 1):
        emit_ln_part2(t_)
    for t_ in (NT - 3, NT - 2, NT - 1):
        emit_ln_part3(t_)

    nwp = [0]
    _pc = {}
    for q in range(4):
        for c in range(NDC):
            w_ = []
            if q == 0 and c == 0:
                w_ = [("act", m_sqln[NT - 1])]
            elif nwp[0] >= 16:
                w_ = [q_win[(nwp[0] - 2) % 4 * 0 + _pc[nwp[0] - 2]]]
            q_win[c * 4 + q] = pl.dma(
                "gpsimd",
                lambda e, c=c, q=q: e.dma_start(
                    out=BW1[:, c * F + q * 1024:c * F + (q + 1) * 1024],
                    in_=d_wint[(c * 4 + q) * P:(c * 4 + q + 1) * P, :]),
                waits=w_, pool="qw")
            _pc[nwp[0]] = c * 4 + q
            nwp[0] += 1
    for j in range(NEC):
        q_wout[j] = pl.dma(
            "gpsimd",
            lambda e, j=j: e.dma_start(out=BW2[:, j * D:(j + 1) * D],
                                       in_=d_woutt[j * P:(j + 1) * P, :]),
            waits=[q_wout[j - 2]] if j >= 2 else [], pool="qw")

    wbt_f32 = SCANBF[:, 2048:2304]
    m_wbt_ld = pl.dma("sync", lambda e: e.dma_start(out=wbt_f32, in_=d_wbt),
                      pool="qx")
    m_wbt = pl.op("vector", lambda e: e.tensor_copy(out=WBT[:], in_=wbt_f32),
                  waits=[m_wbt_ld, ("act", m_sqln[16]), ("dve", m_xnb[16])],
                  inc="dve")
    m_ones2 = pl.chain("vector", [
        lambda e: e.memset(ONESR[:], 1.0),
        lambda e: e.memset(ONESRF[:], 1.0),
    ], waits=[("act", m_sqln[16]), ("dve", m_xnb[16])])

    emit_gamma_load(0)


    DGM0 = 4096

    def dgm_ops():
        fns = []
        for ec in range(NEC):
            fns.append(lambda e, ec=ec: e.tensor_scalar(
                out=B(DGM0 + ec * P, P), in0=IDN[:], scalar1=DFM[:, ec:ec + 1],
                scalar2=None, op0=ALU.mult))
        return fns
    m_dgm = pl.chain("vector", dgm_ops(), waits=[mk_idn, mk_dfm])

    qpieces = [("fc", c, j) for c in range(NDC) for j in range(4)] + \
              [("pj", c, 0) for c in range(NHC)]
    n_qp = len(qpieces)
    m_qin = {}
    m_qcmp = {}
    m_qst = {}
    q_pending = []
    wq_ping = [MQ, MQ + 1024, MQ + 2048]
    NWQ = 3
    thr_emitted = [False]
    m_thr = [None]

    def emit_thr():
        m_gred = pl.chain("vector", [
            lambda e: e.reduce_sum(out=GACC[:, 0:1], in_=GPARTS[:, 0:32], axis=AX.X),
            lambda e: e.reduce_sum(out=GACC[:, 1:2], in_=GPARTS[:, 32:64], axis=AX.X),
        ], waits=[("act", m_gabs[n_gp - 1]), ("dve", d_memset)])
        mm1t = pl.op("tensor",
                     lambda e: e.matmul(PS[7][0:1, 0:2], ONESCF[:], GACC[:],
                                        start=True, stop=True),
                     waits=[("dve", m_gred)], inc="pe")

        mt_ = pl.chain("vector", [
            lambda e: e.tensor_copy(out=THRTMP[:], in_=PS[7][0:1, 0:2]),
            lambda e: e.tensor_scalar(out=THRTMP[:], in0=THRTMP[:], scalar1=1.0 / NELEM_W,
                                      scalar2=1e-6, op0=ALU.mult, op1=ALU.add),
            lambda e: e.tensor_scalar(out=THRROW[0:1, 0:2], in0=THRTMP[:], scalar1=0.5,
                                      scalar2=None, op0=ALU.mult),
            lambda e: e.tensor_scalar(out=THRROW[0:1, 2:4], in0=THRTMP[:], scalar1=-0.5,
                                      scalar2=None, op0=ALU.mult),
        ], waits=[("pe", mm1t)])
        mb = pl.op("tensor",
                   lambda e: e.matmul(PS[7][:, 0:4], ONESRF[:], THRROW[:],
                                      start=True, stop=True),
                   waits=[("dve", mt_)], inc="pe")
        m_thr[0] = pl.op("vector", lambda e: e.tensor_copy(out=THR[:], in_=PS[7][:, 0:4]),
                         waits=[("pe", mb)], inc="dve")

    def emit_quant_piece(i):
        if not thr_emitted[0]:
            emit_thr()
            thr_emitted[0] = True
        kind, c, j = qpieces[i]
        if kind == "fc":
            pk = c * 4 + j
            src = d_wfct[pk * P:(pk + 1) * P, :]
            dst = d_wq1[pk * P:(pk + 1) * P, :]
            thr_c, nthr_c = 0, 2
        else:
            src = d_wpjt[c * P:(c + 1) * P, :]
            dst = d_wq2[c * P:(c + 1) * P, :]
            thr_c, nthr_c = 1, 3
        k, qin, kwaits = xf_alloc()
        wq = BW2[:, wq_ping[i % NWQ]:wq_ping[i % NWQ] + 1024]
        m_qin[i] = pl.dma("sync", lambda e, d=qin, s=src: e.dma_start(out=d, in_=s),
                          waits=kwaits, pool="qx")
        nn_ = BW2[:, MQ + 3072:MQ + 4096]
        m_qcmp[i] = pl.chain("vector", [
            lambda e, qin=qin, nthr_c=nthr_c: e.tensor_single_scalar(
                out=nn_, in_=qin, scalar=THR[:, nthr_c:nthr_c + 1], op=ALU.is_le),
            lambda e, qin=qin, wq=wq, thr_c=thr_c: e.scalar_tensor_tensor(
                out=wq, in0=qin, scalar=THR[:, thr_c:thr_c + 1], in1=nn_,
                op0=ALU.is_ge, op1=ALU.subtract),
        ], waits=[("dve", m_thr[0]), m_qin[i], m_qst.get(i - NWQ)])
        xf_cons[k] = [("dve", m_qcmp[i])]
        q_pending.append((i, dst, wq))

    def emit_quant_store():
        if not q_pending:
            return
        i, dst, wq = q_pending.pop(0)
        m_qst[i] = pl.dma("sync", lambda e, d=dst, s=wq: e.dma_start(out=d, in_=s),
                          waits=[("dve", m_qcmp[i])], pool="qo")

    XPJ0, SZT0, SST0, YT0 = 0, 4096, 8192, 9216
    m_mm1 = {}
    m_evac = {}
    m_mm2 = {}
    m_btev = {}
    m_scan = {}
    m_mm3 = {}
    m_y = {}
    m_mm4 = {}
    m_sevac = {}
    m_sevb = {}
    m_sdma = {}
    prev_sdma = None
    qp_next = 0
    QTICK_FC = tuple(range(1, 16, 2))
    q_wq1 = {}

    def emit_wq1_reload(j):
        for c in range(NDC):
            q_wq1[c * 4 + j] = ld(
                BW1[:, c * F + j * 1024: c * F + (j + 1) * 1024],
                d_wq1[(c * 4 + j) * P:(c * 4 + j + 1) * P, :],
                waits=[("pe", m_mm1[(NG - 1, 8 * j + 7)]), m_qst[c * 4 + j]])

    def nfc_of(g):
        return NEC if g == 0 else NFC

    for g in range(NG):
        gc = g * GRP
        nfc_g = nfc_of(g)
        for fc in range(nfc_g):
            bank = PS[fc % 4]

            def mm1(e, fc=fc, gc=gc, bank=bank):
                last = None
                for c in range(NDC):
                    last = e.matmul(bank[:, 0:GRP],
                                    BW1[:, c * F + fc * P: c * F + (fc + 1) * P],
                                    XNT[:, c * TTOK + gc: c * TTOK + gc + GRP],
                                    start=(c == 0), stop=(c == NDC - 1))
                return last
            if fc >= 4:
                prev_ev = m_evac.get((g, fc - 4))
            elif g >= 1:
                prev_ev = m_evac.get((g - 1, nfc_of(g - 1) - 4 + fc))
            else:
                prev_ev = None
            w = [q_win[c * 4 + fc // 8] for c in range(NDC)] + \
                [("act", prev_ev),
                 ("dve", m_sevb.get((g - 1, fc)) if fc < 4 else None),
                 ("dve", m_trev_d.get(2 * g + 1)),
                 ("act", m_trev_s.get(2 * g + 1)),
                 ("dve", m_trev_d[NT - 1]) if g == 0 and fc < 4 else None,
                 ("act", m_trev_s[NT - 1]) if g == 0 and fc < 4 else None]
            m_mm1[(g, fc)] = pl.op("tensor", mm1, waits=w, inc="pe")

            tgt = XPJ0 + fc * GRP if fc < NEC else SZT0 + (fc - NEC) * GRP

            def ev(e, fc=fc, tgt=tgt, bank=bank):
                return e.activation(out=W2u(tgt, GRP), in_=bank[:, 0:GRP], func=AF.Silu,
                                    bias=BIN[:, fc:fc + 1], scale=1.0)
            ew = [("pe", m_mm1[(g, fc)]), mk_bin]
            if g >= 1:
                if fc < NEC:
                    ew.append(("pe", m_mm4.get((g - 1, NEC - 1), m_mm2.get(g - 1))))
                else:
                    ew.append(("dve", m_y.get((g - 1, fc - NEC))))
            m_evac[(g, fc)] = pl.op("scalar", ev, waits=ew, inc="act")
            if 1 <= g <= 4 and fc % 2 == 0:
                emit_gamma_tick()
            if g >= 5 and fc in QTICK_FC and qp_next < n_qp:
                emit_quant_piece(qp_next)
                qp_next += 1
                if len(q_pending) > 2:
                    emit_quant_store()
            if g == NG - 1 and fc in (8, 16, 24):
                emit_wq1_reload(fc // 8 - 1)

        def mm2(e):
            last = None
            for ec in range(NEC):
                last = e.matmul(PS[4][0:S, 0:GRP], WBT[:, ec * S:(ec + 1) * S],
                                W2u(XPJ0 + ec * GRP, GRP),
                                start=(ec == 0), stop=(ec == NEC - 1))
            return last
        m_mm2[g] = pl.op("tensor", mm2,
                         waits=[("act", m_evac[(g, NEC - 1)]),
                                ("dve", m_wbt),
                                ("dve", m_trev_d[NT - 1]) if g == 0 else None,
                                ("act", m_trev_s[NT - 1]) if g == 0 else None,
                                ("dve", m_btev.get(g - 1))],
                         inc="pe")
        if g == 0:
            def btev(e, gc=gc):
                return e.tensor_scalar(out=BTb(gc, GRP), in0=PS[4][0:S, 0:GRP],
                                       scalar1=0.1, scalar2=MASK[:], op0=ALU.mult,
                                       op1=ALU.mult)
        else:
            def btev(e, gc=gc):
                return e.tensor_scalar(out=BTb(gc, GRP), in0=PS[4][0:S, 0:GRP],
                                       scalar1=0.1, scalar2=None, op0=ALU.mult)
        m_btev[g] = pl.op("vector", btev,
                          waits=[("pe", m_mm2[g]), q_mask], inc="dve")
        if g == 0:
            def scan(e, gc=gc):
                return e.tensor_tensor_scan(out=HSTb(gc, GRP), data0=DEC9[:, 0:GRP],
                                            data1=BTb(gc, GRP), initial=0.0,
                                            op0=ALU.mult, op1=ALU.add)
        else:
            def scan(e, gc=gc):
                return e.tensor_tensor_scan(out=HSTb(gc, GRP), data0=DEC9[:, 0:GRP],
                                            data1=BTb(gc, GRP), initial=HSTb(gc - 1, 1),
                                            op0=ALU.mult, op1=ALU.add)
        m_scan[g] = pl.op("vector", scan, waits=[("dve", m_btev[g])], inc="dve")

        if g >= 1:
            def emit_mm3(ec):
                bank = PS[5 + ec % 2]

                def mm3(e, gc=gc, ec=ec, bank=bank):
                    e.matmul(bank[:, 0:GRP], WCTb(ec), HSTb(gc, GRP),
                             start=True, stop=False)
                    return e.matmul(bank[:, 0:GRP], B(DGM0 + ec * P, P),
                                    W2u(XPJ0 + ec * GRP, GRP),
                                    start=False, stop=True)
                prev_y = m_y.get((g, ec - 2)) if ec >= 2 else m_y.get((g - 1, 14 + ec))
                m_mm3[(g, ec)] = pl.op("tensor", mm3,
                                       waits=[("dve", m_scan[g]), q_small,
                                              ("dve", prev_y), ("dve", m_dgm),
                                              ("dve", m_trev_d[NT - 1])
                                              if g == 1 and ec < 2 else None,
                                              ("act", m_trev_s[NT - 1])
                                              if g == 1 and ec < 2 else None,
                                              ("act", m_evac[(g, ec)])],
                                       inc="pe")
                m_y[(g, ec)] = pl.op(
                    "vector",
                    lambda e, ec=ec, bank=bank: e.tensor_tensor(
                        out=W2u(XPJ0 + ec * GRP, GRP), in0=bank[:, 0:GRP],
                        in1=W2u(SZT0 + ec * GRP, GRP), op=ALU.mult),
                    waits=[("pe", m_mm3[(g, ec)]), ("act", m_evac[(g, NEC + ec)])],
                    inc="dve")

            emit_mm3(0)
            emit_mm3(1)

            def mm4bias(e):
                last = None
                for bb in range(4):
                    n = bb % 2
                    last = e.matmul(PS[bb][:], ONESR[:],
                                    BOUTBF[0:1, n * 512:(n + 1) * 512],
                                    start=True, stop=False)
                return last
            m_mm4b = pl.op("tensor", mm4bias,
                           waits=[("act", m_evac[(g, NFC - 1)]), q_small,
                                  ("dve", m_ones2),
                                  ("dve", m_sevac.get((g - 1, 1)))],
                           inc="pe")
            for ec in range(NEC):
                def mm4ec(e, ec=ec):
                    last = None
                    for tt in range(2):
                        for n in range(2):
                            last = e.matmul(PS[tt * 2 + n][:],
                                            W2u(XPJ0 + ec * GRP + tt * P, P),
                                            BW2[:, ec * D + n * 512: ec * D + (n + 1) * 512],
                                            start=False, stop=(ec == NEC - 1))
                    return last
                m_mm4[(g, ec)] = pl.op("tensor", mm4ec,
                                       waits=[("dve", m_y[(g, ec)]),
                                              q_wout[ec]],
                                       inc="pe")
                if ec + 2 < NEC:
                    emit_mm3(ec + 2)
                if g >= 5 and ec % 2 == 0 and qp_next < n_qp:
                    emit_quant_piece(qp_next)
                    qp_next += 1
                    if len(q_pending) > 2:
                        emit_quant_store()
            for tt in range(2):
                sslot = (lambda c0, n: W2u(SST0 + c0, n)) if tt == 0 else \
                        (lambda c0, n: SCANB[:, 4608 + c0: 4608 + c0 + n])
                for n_ in range(2):
                    bb = tt * 2 + n_
                    m_sevb[(g, bb)] = pl.op(
                        "vector",
                        lambda e, n_=n_, bb=bb, sslot=sslot: e.tensor_copy(
                            out=sslot(n_ * 512, 512), in_=PS[bb][:]),
                        waits=[("pe", m_mm4[(g, NEC - 1)]),
                               m_sdma.get((g - 1, tt)) if n_ == 0 else None],
                        inc="dve")
                m_sevac[(g, tt)] = m_sevb[(g, tt * 2 + 1)]
                mt = (g - 1) * 2 + tt
                m_sdma[(g, tt)] = pl.dma(
                    "sync",
                    lambda e, mt=mt, sslot=sslot: e.dma_start(
                        out=d_sspill[mt * P:(mt + 1) * P, :], in_=sslot(0, 1024)),
                    waits=[("dve", m_sevac[(g, tt)])], pool="qo")
                prev_sdma = m_sdma[(g, tt)]

    while qp_next < n_qp:
        emit_quant_piece(qp_next)
        qp_next += 1
    while q_pending:
        emit_quant_store()

    m_mm4_last = m_mm4[(NG - 1, NEC - 1)]
    emit_wq1_reload(3)

    def w2_waits(c):
        w = [("pe", m_mm4[(NG - 1, c)] if c < NEC else m_mm4_last),
             m_qst[NFC + c]]
        if 16 <= c <= 18:
            w += [("act", m_gabs[n_gp - 1]),
                  m_qst[n_qp - 1], m_qst[n_qp - 2], m_qst[n_qp - 3]]
        elif c == 19:
            w.append(("dve", m_qcmp[n_qp - 1]))
        elif 20 <= c <= 28:
            w += [("dve", m_sevac[(NG - 1, 0)]), m_sdma[(NG - 1, 0)]]
        return w
    q_w2 = [pl.dma("sync",
                   lambda e, c=c: e.dma_start(out=BW2[:, c * D:(c + 1) * D],
                                              in_=d_wq2[c * P:(c + 1) * P, :]),
                   waits=w2_waits(c), pool="qx")
            for c in range(NHC)]


    m_bcast_done_box = [None]

    def emit_f1_prep():
        m_v2ev = pl.chain("vector", [
            lambda e: e.tensor_copy(out=VROW[0:1, 0:512], in_=PS[7][0:1, 0:512]),
            lambda e: e.tensor_copy(out=VROW[32:33, 512:1024], in_=PS[7][32:33, 0:512]),
        ], waits=[("pe", m_v2f[NHC - 1])])
        m_v2mv = ("qmv", pl.op("sync",
                               lambda e: e.dma_start(out=VROW[0:1, 512:1024],
                                                     in_=VROW[32:33, 512:1024]),
                               waits=[("dve", m_v2ev)], inc="qmv", amt=16))
        m_v2bca = pl.op("tensor",
                        lambda e: e.matmul(PS[7][:], ONESRF[:], VROW[0:1, 0:512],
                                           start=True, stop=True),
                        waits=[m_v2mv, ("dve", m_ones2)], inc="pe")
        m_v2eva = pl.op("vector", lambda e: e.tensor_copy(out=V2B[:, 0:512], in_=PS[7][:]),
                        waits=[("pe", m_v2bca), ("dve", m_qcmp[n_qp - 1])],
                        inc="dve")
        m_v2bcb = pl.op("tensor",
                        lambda e: e.matmul(PS[7][:], ONESRF[:], VROW[0:1, 512:1024],
                                           start=True, stop=True),
                        waits=[("dve", m_v2eva)], inc="pe")
        m_v2evb = pl.op("vector", lambda e: e.tensor_copy(out=V2B[:, 512:1024], in_=PS[7][:]),
                        waits=[("pe", m_v2bcb)], inc="dve")

        def cs2mm(e):
            last = None
            for c in range(NHC):
                e.matmul(PS[7][0:1, 0:512], ONESC[:], BW2[:, c * D:c * D + 512],
                         start=(c == 0), stop=(c == NHC - 1))
                last = e.matmul(PS[7][32:33, 0:512], ONESC[:],
                                BW2[:, c * D + 512:(c + 1) * D],
                                start=(c == 0), stop=(c == NHC - 1))
            return last
        m_cs2 = pl.op("tensor", cs2mm,
                      waits=[("dve", m_fold[NHC - 1]), ("dve", m_v2evb)], inc="pe")
        m_cs2ev = pl.chain("vector", [
            lambda e: e.tensor_copy(out=VROW[0:1, 0:512], in_=PS[7][0:1, 0:512]),
            lambda e: e.tensor_copy(out=VROW[32:33, 512:1024], in_=PS[7][32:33, 0:512]),
        ], waits=[("pe", m_cs2)])
        m_cs2mv = ("qmv", pl.op("sync",
                                lambda e: e.dma_start(out=VROW[0:1, 512:1024],
                                                      in_=VROW[32:33, 512:1024]),
                                waits=[("dve", m_cs2ev)], inc="qmv", amt=16))
        m_cs2bca = pl.op("tensor",
                         lambda e: e.matmul(PS[7][:], ONESRF[:], VROW[0:1, 0:512],
                                            start=True, stop=True),
                         waits=[("dve", m_cs2ev)], inc="pe")
        m_cs2eva = pl.op("vector", lambda e: e.tensor_copy(out=CS2B[:, 0:512], in_=PS[7][:]),
                         waits=[("pe", m_cs2bca), m_sdma[(NG - 1, 1)]],
                         inc="dve")
        m_cs2bcb = pl.op("tensor",
                         lambda e: e.matmul(PS[7][:], ONESRF[:], VROW[0:1, 512:1024],
                                            start=True, stop=True),
                         waits=[m_cs2mv, ("dve", m_cs2eva)], inc="pe")
        m_bcast_done_box[0] = pl.op("vector",
                                    lambda e: e.tensor_copy(out=CS2B[:, 512:1024],
                                                            in_=PS[7][:]),
                                    waits=[("pe", m_cs2bcb)], inc="dve")

    HGT0, HGEL0, RAWG0, SQD2 = 0, 4096, 5120, 6144
    m_mm5 = {}
    m_gelu = {}
    m_sq = {}
    m_trf = {}
    m_trfev = {}
    m_mm6 = {}
    m_rawev = {}
    m_rawdma = {}
    m_stats_red = {}
    PREV3 = {0: 6, 1: 7, 2: 5}

    def emit_mm5(m, j):
        tcol = HALO + m * P
        bank = PS[j % 3]
        prev_g = m_gelu.get((m, j - 3)) if j >= 3 else m_gelu.get((m - 1, PREV3[j]))

        def mm5(e, j=j, tcol=tcol, bank=bank):
            last = None
            for c in range(NDC):
                last = e.matmul(bank[:],
                                XNT[:, c * TTOK + tcol: c * TTOK + tcol + P],
                                BW1[:, c * F + j * 512: c * F + (j + 1) * 512],
                                start=(c == 0), stop=(c == NDC - 1))
            return last
        psw = ("dve", m_sevb[(NG - 1, j)]) if m == 0 and j < 3 else None
        m_mm5[(m, j)] = pl.op("tensor", mm5,
                              waits=[q_wq1[c * 4 + j // 2] for c in range(NDC)]
                              + [("act", prev_g), psw], inc="pe")

    def emit_gelu_sq(m, j):
        bank = PS[j % 3]
        prev_tr = m_trf.get((m, j - 2)) if j >= 2 else m_trf.get((m - 1, 6 + j))
        prev_sq = m_sq.get((m, j - 2)) if j >= 2 else m_sq.get((m - 1, 6 + j))

        def gl(e, j=j, bank=bank):
            return e.activation(out=B(HGEL0 + 512 * (j % 2), 512), in_=bank[:],
                                func=AF.Gelu, accum_out=GS[:, j:j + 1])
        m_gelu[(m, j)] = pl.op("scalar", gl,
                               waits=[("pe", m_mm5[(m, j)]), ("pe", prev_tr),
                                      ("dve", prev_sq),
                                      ("dve", m_stats_red.get(m - 1))],
                               inc="act")

        def sq(e, j=j):
            hg = B(HGEL0 + 512 * (j % 2), 512)
            return e.scalar_tensor_tensor(out=B(SQD2, 512), in0=hg, scalar=1.0,
                                          in1=hg, op0=ALU.mult, op1=ALU.mult,
                                          accum_out=QS[:, j:j + 1])
        prev_sq2 = m_sq.get((m, j - 1)) if j >= 1 else m_sq.get((m - 1, 7))
        m_sq[(m, j)] = pl.op("vector", sq,
                             waits=[("act", m_gelu[(m, j)]),
                                    ("dve", prev_sq2)], inc="dve")

    def emit_trf(m, j):
        prev_trev = m_trfev.get((m, j - 2)) if j >= 2 else m_trfev.get((m - 1, 6 + j))

        def trf(e, j=j):
            hg = B(HGEL0 + 512 * (j % 2), 512)
            bankt = PS[3 + j % 2]
            last = None
            for b in range(4):
                last = e.matmul(bankt[:, b * P:(b + 1) * P], hg[:, b * P:(b + 1) * P],
                                IDN[:], start=True, stop=True)
            return last
        m_trf[(m, j)] = pl.op("tensor", trf,
                              waits=[("act", m_gelu[(m, j)]), ("dve", prev_trev),
                                     ("dve", m_sevb[(NG - 1, 3)])
                                     if m == 0 and j < 2 else None],
                              inc="pe")
        m_trfev[(m, j)] = pl.op(
            "vector",
            lambda e, j=j: e.tensor_copy(out=B(HGT0 + j * 512, 512),
                                         in_=PS[3 + j % 2][:]),
            waits=[("pe", m_trf[(m, j)]), ("pe", m_mm6.get((m - 1, 1)))],
            inc="dve")

    m_v2f = {}
    m_fold = {}

    def emit_v2_fold(c):
        def v2mm(e, c=c):
            e.matmul(PS[7][0:1, 0:512], LNBbf[:, c:c + 1], BW2[:, c * D:c * D + 512],
                     start=(c == 0), stop=(c == NHC - 1))
            return e.matmul(PS[7][32:33, 0:512], LNBbf[:, c:c + 1],
                            BW2[:, c * D + 512:(c + 1) * D],
                            start=(c == 0), stop=(c == NHC - 1))
        m_v2f[c] = pl.op("tensor", v2mm,
                         waits=[q_w2[c], q_lnbbf], inc="pe")
        m_fold[c] = pl.op(
            "vector",
            lambda e, c=c: e.tensor_scalar(
                out=BW2[:, c * D:(c + 1) * D], in0=BW2[:, c * D:(c + 1) * D],
                scalar1=LNW[:, c:c + 1], scalar2=None, op0=ALU.mult),
            waits=[("pe", m_v2f[c]), mk_lnw], inc="dve")

    m_f2b_t = {}
    m_f3_sld = {}
    m_f3_gld = {}
    m_f3_xld = {}
    m_f3_o1 = {}
    m_f3_tanh = {}
    m_f3_fin = {}
    m_f3_odma = {}
    f3_next = [0]
    pending_tanh = []

    def emit_f2_cols(c0, c1):
        s4 = slice(c0, c1)
        mu, sq = MUT[:, s4], SQT[:, s4]
        vet, nta = VET[:, s4], NTA[:, s4]
        rst, nrm = RST[:, s4], NRM[:, s4]
        a = pl.chain("vector", [
            lambda e: e.tensor_scalar(out=mu, in0=mu, scalar1=1.0 / HH,
                                      scalar2=None, op0=ALU.mult),
            lambda e: e.tensor_scalar(out=sq, in0=sq, scalar1=1.0 / HH,
                                      scalar2=None, op0=ALU.mult),
            lambda e: e.tensor_tensor(out=nta, in0=mu, in1=mu, op=ALU.mult),
            lambda e: e.tensor_tensor(out=vet, in0=sq, in1=nta, op=ALU.subtract),
            lambda e: e.tensor_scalar(out=vet, in0=vet, scalar1=1e-5,
                                      scalar2=None, op0=ALU.add),
        ], waits=[("dve", m_stats_red[c1 - 1])])
        sd = pl.op("scalar", lambda e: e.activation(out=nta, in_=vet, func=AF.Sqrt),
                   waits=[("dve", a)], inc="act")
        mk = pl.chain("vector", [
            lambda e: e.reciprocal(out=rst, in_=nta),
            lambda e: e.tensor_tensor(out=nta, in0=rst, in1=rst, op=ALU.mult),
            lambda e: e.tensor_tensor(out=nta, in0=nta, in1=vet, op=ALU.mult),
            lambda e: e.tensor_scalar(out=nta, in0=nta, scalar1=-0.5, scalar2=1.5,
                                      op0=ALU.mult, op1=ALU.add),
            lambda e: e.tensor_tensor(out=rst, in0=rst, in1=nta, op=ALU.mult),
            lambda e: e.tensor_tensor(out=nta, in0=mu, in1=rst, op=ALU.mult),
            lambda e: e.tensor_scalar(out=nrm, in0=nta, scalar1=-1.0,
                                      scalar2=None, op0=ALU.mult),
        ], waits=[("act", sd)])
        for t in range(c0, c1):
            m_f2b_t[t] = mk

    def f3_sbuf(t):
        if t % 2 == 0:
            return F3S[:, 0:1024], F3S[:, 1024:2048]
        return SCANB[:, 2048:3072], SCANB[:, 3072:4096]

    def emit_f3_loads(t):
        if t in m_f3_sld or t >= NMT:
            return
        sre_t, gre_t = f3_sbuf(t)
        bw = ("act", m_sqln[16 + t]) if t < 2 else None
        bwd = ("dve", m_xnb[16 + t]) if t < 2 else None
        bw2 = ("dve", m_ones2) if t == 0 else None
        m_f3_sld[t] = pl.dma(
            "sync",
            lambda e, t=t, d=sre_t: e.dma_start(out=d, in_=d_sspill[t * P:(t + 1) * P, :]),
            waits=[m_sdma[(t // 2 + 1, t % 2)], bw, bwd, bw2,
                   ("dve", m_f3_fin.get(2 * (t - 2) + 1))], pool="qx")
        m_f3_gld[t] = pl.dma(
            "sync",
            lambda e, t=t, d=gre_t: e.dma_start(out=d, in_=d_gspill[t * P:(t + 1) * P, :]),
            waits=[m_rawdma[t], bw, bwd,
                   ("dve", m_f3_o1.get(2 * (t - 2) + 1))], pool="qx")

    def emit_f3_xld(t):
        if t in m_f3_xld or t >= NMT:
            return
        m_f3_xld[t] = pl.dma(
            "sync",
            lambda e, t=t: e.dma_start(out=F3X[:, 0:1024],
                                       in_=d_x[(t + 2) * P:(t + 3) * P, :]),
            waits=[("dve", m_f3_fin.get(2 * (t - 1) + 1)),
                   ("dve", m_thr[0]) if t == 0 else None,
                   ("act", m_gabs[n_gp - 1]) if t == 0 else None,
                   ("dve", m_ones2) if t == 0 else None], pool="qx")

    def emit_f3_o1(i):
        m, hv = i // 2, i % 2
        if hv == 0:
            emit_f3_loads(m)
            emit_f3_loads(m + 1)
            emit_f3_xld(m)
        sa = Fv(512 * (i % 2), 512)
        sb2 = Fv(512 * ((i + 1) % 2), 512)
        sre_t, gre_t = f3_sbuf(m)
        greh = gre_t[:, hv * 512:(hv + 1) * 512]
        m_f3_o1[i] = pl.chain("vector", [
            lambda e, m=m, hv=hv, sa=sa: e.scalar_tensor_tensor(
                out=sa, in0=CS2B[:, hv * 512:(hv + 1) * 512],
                scalar=NRM[:, m:m + 1], in1=V2B[:, hv * 512:(hv + 1) * 512],
                op0=ALU.mult, op1=ALU.add),
            lambda e, m=m, greh=greh, sa=sa: e.scalar_tensor_tensor(
                out=sa, in0=greh, scalar=RST[:, m:m + 1], in1=sa,
                op0=ALU.mult, op1=ALU.add),
        ], waits=[("dve", m_f2b_t[m]), m_f3_gld[m],
                  ("dve", m_bcast_done_box[0])])
        pending_tanh.append(i)

    def emit_f3_tanh(i):
        sa = Fv(512 * (i % 2), 512)
        sb2 = Fv(512 * ((i + 1) % 2), 512)
        m_f3_tanh[i] = pl.op(
            "scalar",
            lambda e, sa=sa, sb2=sb2: e.activation(out=sb2, in_=sa, func=AF.Tanh,
                                                   scale=0.5),
            waits=[("dve", m_f3_o1[i])], inc="act")

    def emit_f3_fin(i):
        m, hv = i // 2, i % 2
        sa = Fv(512 * (i % 2), 512)
        sb2 = Fv(512 * ((i + 1) % 2), 512)
        ob = Fv(1024 + hv * 512, 512)
        sre_t, _ = f3_sbuf(m)
        sreh = sre_t[:, hv * 512:(hv + 1) * 512]
        xreh = F3X[:, hv * 512:(hv + 1) * 512]
        m_f3_fin[i] = pl.chain("vector", [
            lambda e, sa=sa, sb2=sb2: e.scalar_tensor_tensor(
                out=sa, in0=sb2, scalar=1.0, in1=sa, op0=ALU.add, op1=ALU.mult),
            lambda e, sa=sa, sb2=sb2, sreh=sreh: e.scalar_tensor_tensor(
                out=sb2, in0=sa, scalar=0.5, in1=sreh, op0=ALU.mult, op1=ALU.mult),
            lambda e, sb2=sb2, ob=ob, xreh=xreh: e.tensor_tensor(
                out=ob, in0=sb2, in1=xreh, op=ALU.add),
        ], waits=[("act", m_f3_tanh[i]), m_f3_sld[m], m_f3_xld[m],
                  m_f3_odma.get(m - 1)])
        if hv == 1:
            m_f3_odma[m] = pl.dma(
                "sync",
                lambda e, m=m: e.dma_start(out=d_out[m * P:(m + 1) * P, :],
                                           in_=Fv(1024, 1024)),
                waits=[("dve", m_f3_fin[i])], pool="qo")

    for m in range(NMT):
        if m == 1:
            emit_f1_prep()
        emit_mm5(m, 0)
        for j in range(8):
            if m == 0:
                for c in range(4 * j, 4 * j + 4):
                    emit_v2_fold(c)
            emit_gelu_sq(m, j)
            if j < 7:
                emit_mm5(m, j + 1)
            emit_trf(m, j)
            if j in (2, 3, 4) and pending_tanh:
                emit_f3_tanh(pending_tanh.pop(0))
        m_stats_red[m] = pl.chain("vector", [
            lambda e, m=m: e.reduce_sum(out=MUT[:, m:m + 1], in_=GS, axis=AX.X),
            lambda e, m=m: e.reduce_sum(out=SQT[:, m:m + 1], in_=QS, axis=AX.X),
        ], waits=[("act", m_gelu[(m, 7)]), ("dve", m_sq[(m, 7)])])
        f3_allowed = 2 * (m - 1) if m >= 13 else (m // 4) * 8
        for _ in range(3):
            if f3_next[0] < f3_allowed:
                i_ = f3_next[0]
                f3_next[0] += 1
                if i_ >= 1:
                    while pending_tanh and pending_tanh[0] <= i_ - 1:
                        emit_f3_tanh(pending_tanh.pop(0))
                    emit_f3_fin(i_ - 1)
                emit_f3_o1(i_)
        for n in range(2):
            def mm6a(e, n=n):
                bank6 = PS[5 + n]
                last = None
                for fc in range(NHC - 4):
                    last = e.matmul(bank6[:], B(HGT0 + fc * P, P),
                                    BW2[:, fc * D + n * 512: fc * D + (n + 1) * 512],
                                    start=(fc == 0), stop=False)
                return last

            def mm6b(e, n=n):
                bank6 = PS[5 + n]
                last = None
                for fc in range(NHC - 4, NHC):
                    last = e.matmul(bank6[:], B(HGT0 + fc * P, P),
                                    BW2[:, fc * D + n * 512: fc * D + (n + 1) * 512],
                                    start=False, stop=(fc == NHC - 1))
                return last
            pl.op("tensor", mm6a,
                  waits=[("dve", m_trfev[(m, 6)]),
                         ("dve", m_fold[NHC - 1]),
                         ("dve", m_rawev.get((m - 1, n)))],
                  inc="pe")
            m_mm6[(m, n)] = pl.op("tensor", mm6b,
                                  waits=[("dve", m_trfev[(m, 7)])],
                                  inc="pe")
            rawb = (B(RAWG0, 1024) if m % 2 == 0
                    else SCANB[:, 4608:5632])
            m_rawev[(m, n)] = pl.op(
                "vector",
                lambda e, n=n, rawb=rawb: e.tensor_copy(
                    out=rawb[:, n * 512:(n + 1) * 512], in_=PS[5 + n][:]),
                waits=[("pe", m_mm6[(m, n)]), m_rawdma.get(m - 2),
                       m_sdma[(NG - 1, 1)] if (m, n) == (1, 0) else None],
                inc="dve")
        m_rawdma[m] = pl.dma(
            "sync",
            lambda e, m=m, rawb=rawb: e.dma_start(
                out=d_gspill[m * P:(m + 1) * P, :], in_=rawb),
            waits=[("dve", m_rawev[(m, 1)])], pool="qo")
        if m % 4 == 3 and m < 12:
            emit_f2_cols(m - 3, m + 1)
        elif m in (13, 15):
            emit_f2_cols(m - 1, m + 1)

    while f3_next[0] < 2 * NMT:
        i_ = f3_next[0]
        f3_next[0] += 1
        if i_ >= 1:
            while pending_tanh and pending_tanh[0] <= i_ - 1:
                emit_f3_tanh(pending_tanh.pop(0))
            emit_f3_fin(i_ - 1)
        emit_f3_o1(i_)
    while pending_tanh:
        emit_f3_tanh(pending_tanh.pop(0))
    emit_f3_fin(2 * NMT - 1)

    if _DEBUG:
        fa = pl.snap()

        def dmp(name, src):
            pl.dma("sync", lambda e, d=dbg[name], s=src: e.dma_start(out=d, in_=s),
                   waits=[("dve", fa["dve"]), ("act", fa["act"]), ("pe", fa["pe"]),
                          ("pool", fa["pool"])],
                   pool="qo")
        dmp("xnt", XNT[:])
        dmp("bt", BTb(0, TTOK))
        dmp("hst", HSTb(0, TTOK))
        dmp("thr", THR[:])
        dmp("v2b", V2B[:])
        dmp("cs2b", CS2B[:])
        dmp("mut", MUT)
        dmp("rst", RST)

    qo_waits = [(s, c) for s, c in pl.snap().items()
                if (s.startswith("qo") or s == "qsm") and c > 0]
    pl.op("sync", None, waits=qo_waits)
    endm = pl.snap()
    pl.barrier(endm)

    with nc.Block() as blk:
        def runner(name):
            stream = pl.stream[name]

            def go(eng):
                for waits, fn, inc in stream:
                    for s_, v in waits:
                        eng.wait_ge(sems[s_], v)
                    if fn is None:
                        continue
                    if isinstance(fn, tuple) and fn[0] == "chain":
                        _, fns, base, csem = fn
                        for j, f in enumerate(fns):
                            f(eng).then_inc(sems[csem], 1)
                            eng.wait_ge(sems[csem], base + j + 1)
                        continue
                    inst = fn(eng)
                    if inc is not None:
                        s_, amt = inc
                        inst.then_inc(sems[s_], amt)
            return go
        blk.sync(runner("sync"))
        blk.gpsimd(runner("gpsimd"))
        blk.tensor(runner("tensor"))
        blk.vector(runner("vector"))
        blk.scalar(runner("scalar"))

    es.close()
    return nc


def _prep_inputs(L0_attn_x, L0_attn_W_in, L0_attn_b_in, L0_attn_W_B, L0_attn_W_C,
                 L0_attn_D, L0_attn_W_out, L0_attn_b_out, L0_ffn_W_fc, L0_ffn_ln_w,
                 L0_ffn_ln_b, L0_ffn_W_proj):
    f32 = np.float32
    x = np.ascontiguousarray(np.asarray(L0_attn_x, f32).reshape(16384, 1024))
    common = dict(
        w_in_t=np.ascontiguousarray(
            np.asarray(L0_attn_W_in, f32).T.reshape(8, 128, 4, 1024)
            .transpose(0, 2, 1, 3).reshape(F, D)),
        w_out_t=np.ascontiguousarray(np.asarray(L0_attn_W_out, f32).T),
        w_fc_t=np.ascontiguousarray(
            np.asarray(L0_ffn_W_fc, f32).T.reshape(8, 128, 4, 1024)
            .transpose(0, 2, 1, 3).reshape(HH, D)),
        w_proj_t=np.ascontiguousarray(np.asarray(L0_ffn_W_proj, f32).T),
        w_b_t=np.ascontiguousarray(
            np.asarray(L0_attn_W_B, f32).T.reshape(NEC, P, S)
            .transpose(1, 0, 2).reshape(P, NEC * S)),
        w_c_t=np.ascontiguousarray(np.asarray(L0_attn_W_C, f32).T),
        par_fm=np.ascontiguousarray(np.concatenate([
            np.asarray(L0_attn_b_in, f32).reshape(NFC, P).T,
            np.asarray(L0_attn_D, f32).reshape(NEC, P).T,
            np.asarray(L0_ffn_ln_w, f32).reshape(NHC, P).T,
        ], axis=1)),
        b_out_row=np.ascontiguousarray(np.asarray(L0_attn_b_out, f32).reshape(1, D)),
        lnb_fm=np.ascontiguousarray(np.asarray(L0_ffn_ln_b, f32).reshape(NHC, P).T),
        ident=np.eye(P, dtype=f32).astype(ml_dtypes.bfloat16),
    )
    in_maps = []
    for c in range(8):
        lo = c * TC
        if c == 0:
            xs = np.concatenate([np.zeros((HALO, D), f32), x[0:TC]], axis=0)
            mask = np.zeros((S, 1), f32)
        else:
            xs = x[lo - HALO: lo + TC]
            mask = np.ones((S, 1), f32)
        m = dict(common)
        m["xin"] = np.ascontiguousarray(xs)
        m["mask16"] = mask
        in_maps.append(m)
    return in_maps


def kernel(**inputs):
    in_maps = _prep_inputs(**inputs)
    nc = build_nc()
    res = run_bass_kernel_spmd(nc, in_maps, core_ids=list(range(8)))
    out = np.concatenate([np.asarray(r["out"], np.float32) for r in res.results], axis=0)
    return out.reshape(1, 16384, 1024)














